# version 04
# baseline (speedup 1.0000x reference)
import numpy as np
import concourse.bass as bass
import concourse.mybir as mybir
from concourse.bass_utils import run_bass_kernel_spmd

F32 = mybir.dt.float32
BF16 = mybir.dt.bfloat16
I32 = mybir.dt.int32
AF = mybir.ActivationFunctionType
ALU = mybir.AluOpType
AX = mybir.AxisListType

D = 1024
HD = 4096
E = 8
S_CORE = 8192
T = 1024
NSUP = S_CORE // T
NTB = T // 128
KD = D // 128
PI = float(np.pi)
INV2PI = float(1.0 / (2 * np.pi))
TWOPI = float(2 * np.pi)
MAGIC = 0x5F3759DF

ENGINES = ("sync", "gpsimd", "scalar", "vector", "tensor")
ESEM = {"sync": "sd", "gpsimd": "gd", "scalar": "se", "vector": "ve", "tensor": "pe"}


class Buf:
    __slots__ = ("name", "ready", "readers")

    def __init__(self, name):
        self.name = name
        self.ready = None
        self.readers = []


class Op:
    __slots__ = ("eng", "fn", "waits", "sem", "n", "need_inc", "val", "async_done")

    def __init__(self, eng, fn, sem, n, async_done):
        self.eng = eng
        self.fn = fn
        self.waits = []
        self.sem = sem
        self.n = n
        self.need_inc = False
        self.val = None
        self.async_done = async_done


class Prog:
    def __init__(self):
        self.q = {e: [] for e in ENGINES}

    def emit(self, eng, fn, reads=(), writes=(), dma=False, sem=None):
        sem = sem or ESEM[eng]
        op = Op(eng, fn, sem, 16 if dma else 1, dma)
        deps = []
        for b in reads:
            if b.ready is not None:
                deps.append(b.ready)
        for w in writes:
            deps.extend(w.readers)
            if w.ready is not None:
                deps.append(w.ready)
        seen = set()
        for d in deps:
            if id(d) in seen:
                continue
            seen.add(id(d))
            op.waits.append(d)
        self.q[eng].append(op)
        for w in writes:
            w.ready = op
            w.readers = []
        for b in reads:
            b.readers.append(op)
        return op

    def finalize(self):
        for q in self.q.values():
            for op in q:
                if op.n == 16:
                    op.need_inc = True
                for d in op.waits:
                    d.need_inc = True
        cnt = {}
        order = []
        for e in ENGINES:
            order.extend(self.q[e])
        for e in ENGINES:
            for op in self.q[e]:
                if op.need_inc:
                    cnt[op.sem] = cnt.get(op.sem, 0) + op.n
                    op.val = cnt[op.sem]
        self.final_counts = cnt


def build_nc(dbg=False, stages=('X','A1','A2','A3','F0','F1','F2'), nsup=NSUP):
    nc = bass.Bass("TRN2", target_bir_lowering=False)

    dr = {}
    def din(name, shape, dtype=F32):
        dr[name] = nc.dram_tensor(name, list(shape), dtype, kind="ExternalInput").ap()
        return dr[name]

    x_h = din("x", (S_CORE, D))
    wk_h = din("wk", (D, D)); wv_h = din("wv", (D, D)); wq_h = din("wq", (D, D))
    wkp_h = din("wkp", (D, D)); wqp_h = din("wqp", (D, D)); woa_h = din("woa", (D, D))
    w1_h = din("w1", (E, D, HD)); w2_h = din("w2", (E, D, HD)); wof_h = din("wof", (E, HD, D))
    rw_p_h = din("rw_p", (128, KD * E))
    bk_p_h = din("bk_p", (128, KD)); bq_p_h = din("bq_p", (128, KD)); bv_p_h = din("bv_p", (128, KD))
    fg_p_h = din("fg_p", (128, KD)); fb_p_h = din("fb_p", (128, KD))
    eg_p_h = din("eg_p", (128, E * KD)); eb_p_h = din("eb_p", (128, E * KD))
    b1_p_h = din("b1_p", (128, E * 32)); b2_p_h = din("b2_p", (128, E * 32))
    bkp_r_h = din("bkp_r", (1, D)); bqp_r_h = din("bqp_r", (1, D)); bo_r_h = din("bo_r", (1, D))
    bof_r_h = din("bof_r", (E, D))
    ps_bc_h = din("ps_bc", (128, D))
    lng_bc_h = din("lng_bc", (128, D)); lnb_bc_h = din("lnb_bc", (128, D))
    fg_bc_h = din("fg_bc", (128, D)); fb_bc_h = din("fb_bc", (128, D))
    ident_h = din("ident", (128, 128))
    ut2_h = din("ut2", (128, 64))
    consts_h = din("consts", (128, 4))
    y_h = nc.dram_tensor("y", [S_CORE, D], F32, kind="ExternalOutput").ap()
    dbg_h = {}
    if dbg:
        for nm, sh, dt in [
            ("d_xn", (128, 8192), BF16), ("d_fm", (128, 8192), BF16),
            ("d_kfm", (128, 8192), BF16), ("d_qfm", (128, 8192), BF16),
            ("d_vtok", (128, 8192), BF16), ("d_gates", (128, 64), F32),
            ("d_stats", (128, 64), F32), ("d_s", (128, 8192), BF16),
            ("d_yfm", (128, 8192), BF16), ("d_acc", (128, 8192), F32),
            ("d_tanhk", (128, 1024), F32), ("d_kp", (128, 1024), F32),
            ("d_sink", (128, 1024), F32), ("d_cosk", (128, 1024), F32),
            ("d_br", (128, 1024), BF16), ("d_ret", (128, 1024), F32),
            ("d_nrm", (128, 1024), BF16), ("d_xf", (128, 1024), F32),
            ("d_h0", (128, 1024), BF16), ("d_z2", (128, 1024), F32),
            ("d_sinq", (128, 1024), F32), ("d_cosq", (128, 1024), F32),
            ("d_mr", (128, 1024), F32),
        ]:
            dbg_h[nm] = nc.dram_tensor(nm, list(sh), dt, kind="ExternalOutput").ap()

    from contextlib import ExitStack
    es = ExitStack()
    with es:
        sb = lambda name, shape, dt=F32: es.enter_context(nc.sbuf_tensor(name, list(shape), dt))
        ps = lambda name, shape, dt=F32: es.enter_context(nc.psum_tensor(name, list(shape), dt))

        g1 = sb("g1", (128, 6 * 1024))
        g1b = g1[:].bitcast(BF16)
        xn_tok = sb("xn_tok", (128, NTB * 1024), BF16)
        fm = sb("fm", (128, KD * 1024), BF16)
        g2 = sb("g2", (128, 8 * 1024))
        g2b = g2[:].bitcast(BF16)
        vx = sb("vx", (128, NTB * 1024), BF16)
        wkb0 = sb("wkb0", (128, KD * 1024), BF16)
        wkb1 = sb("wkb1", (128, KD * 1024), BF16)
        wkb1f = wkb1[:].bitcast(F32)
        s_sb = sb("s_sb", (128, NTB * 1024), BF16)
        xio0 = sb("xio0", (128, 1024)); xio1 = sb("xio1", (128, 1024))
        htmp = sb("htmp", (128, 1024))
        htmp2 = sb("htmp2", (128, 1024))
        vfm = sb("vfm", (128, 2 * 1024), BF16)
        stats = sb("stats", (128, 64))
        gates = sb("gates", (128, NTB * E))
        gatesT = sb("gatesT", (8, 1024), BF16)
        gbf = sb("gbf", (128, 8), BF16)
        bkp_hi = sb("bkp_hi", (1, D), BF16); bkp_lo = sb("bkp_lo", (1, D), BF16)
        bqp_hi = sb("bqp_hi", (1, D), BF16); bqp_lo = sb("bqp_lo", (1, D), BF16)
        bo_hi = sb("bo_hi", (1, D), BF16); bo_lo = sb("bo_lo", (1, D), BF16)
        bof_bf = sb("bof_bf", (E, D), BF16)
        ps_bc = sb("ps_bc_s", (128, D))
        lng_bc = sb("lng_bc_s", (128, D), BF16); lnb_bc = sb("lnb_bc_s", (128, D), BF16)
        fg_bc = sb("fg_bc_s", (128, D), BF16); fb_bc = sb("fb_bc_s", (128, D), BF16)
        bk_p = sb("bk_p_s", (128, KD)); bq_p = sb("bq_p_s", (128, KD)); bv_p = sb("bv_p_s", (128, KD))
        fg_p = sb("fg_p_s", (128, KD)); fb_p = sb("fb_p_s", (128, KD))
        fb_cb = sb("fb_cb", (128, KD), BF16)
        eg_p = sb("eg_p_s", (128, E * KD)); eb_p = sb("eb_p_s", (128, E * KD))
        b1_p = sb("b1_p_s", (128, E * 32)); b2_p = sb("b2_p_s", (128, E * 32))
        rw_p = sb("rw_p_s", (128, KD * E))
        rw_bf = sb("rw_bf", (128, KD * E), BF16)
        rwg = sb("rwg", (128, KD * E), BF16)
        brw_row = sb("brw_row", (1, E)); brw_rb = sb("brw_rb", (1, E), BF16)
        brw_bc = sb("brw_bc", (128, E))
        ident_b = sb("ident_b", (128, 128), BF16)
        ut2_b = sb("ut2_b", (128, 64), BF16)
        ones_c = sb("ones_c", (1, 128), BF16)
        cs = sb("cs", (128, 4))

        pm1 = ps("pm1", (128, 1024))
        pm2 = ps("pm2", (128, 1024))
        pm3 = ps("pm3", (128, 1024))
        ps1 = ps("ps1", (128, 512))
        pt = ps("pt", (128, 1024), BF16)
        pt32 = pt[:].bitcast(F32)

        sems = {}
        for s in ("sd", "gd", "se", "ve", "pe", "so"):
            sems[s] = es.enter_context(nc.semaphore(s))

        nc.const_aps.register(F32, PI / 2, cs[:, 0:1])
        nc.const_aps.register(F32, 0.5, cs[:, 1:2])

        P = Prog()
        B = lambda n: Buf(n)
        b_xn_tok = B("xn_tok"); b_fm = B("fm"); b_kfm = B("kfm"); b_qfm = B("qfm")
        b_vx = B("vx"); b_wkb0 = B("wkb0"); b_wkb1 = B("wkb1"); b_s = B("s")
        b_xio = [B("xio0"), B("xio1")]; b_htmp = B("htmp")
        b_vfm = [B("vfm0"), B("vfm1")]
        b_stats = B("stats"); b_gates = B("gates"); b_gatesT = B("gatesT"); b_gbf = B("gbf")
        b_acc = B("acc")
        b_slot = [B(f"S{i}") for i in range(6)]
        b_pm1 = B("pm1"); b_pm2 = B("pm2"); b_pm3 = B("pm3"); b_ps1 = B("ps1"); b_pt = B("pt")
        b_consts = B("consts")
        b_w1b = [B("w1b0"), B("w1b1")]; b_w2b = [B("w2b0"), B("w2b1")]
        b_wob = [B("wob0"), B("wob1")]
        b_z2 = B("z2"); b_sl = B("sl")
        b_z2o = B("z2o"); b_slo = B("slo"); b_htmp2 = B("htmp2")
        b_h = [B(f"h{i}") for i in range(8)]
        b_y = B("y_dummy")

        SLOT = [g1[:, i * 1024:(i + 1) * 1024] for i in range(6)]
        SLOTB = [g1b[:, i * 2048:(i + 1) * 2048] for i in range(6)]
        h_v = g1b[:].rearrange("p (hc t) -> p hc t", hc=12)
        xn_v = xn_tok[:].rearrange("p (tb t) -> p tb t", tb=NTB)
        fm_v = fm[:].rearrange("p (k t) -> p k t", k=KD)
        kfm_v = g2b[:, 0:KD * 1024].rearrange("p (k t) -> p k t", k=KD)
        qfm_v = g2b[:, KD * 1024:2 * KD * 1024].rearrange("p (k t) -> p k t", k=KD)
        acc_v = g2[:].rearrange("p (tb t) -> p tb t", tb=NTB)
        vtok_v = vx[:].rearrange("p (tb t) -> p tb t", tb=NTB)
        xe_v = vx[:].rearrange("p (k t) -> p k t", k=KD)
        ytmp = vx[:, 0:1024]
        s_v = s_sb[:].rearrange("p (tb t) -> p tb t", tb=NTB)
        wkb0_v = wkb0[:].rearrange("p (k t) -> p k t", k=KD)
        wkb1_v = wkb1[:].rearrange("p (k t) -> p k t", k=KD)
        w1b_v = [wkb0[:, 0:2048].rearrange("p (k h) -> p k h", k=KD),
                 wkb0[:, 2048:4096].rearrange("p (k h) -> p k h", k=KD)]
        w2b_v = [wkb0[:, 4096:6144].rearrange("p (k h) -> p k h", k=KD),
                 wkb0[:, 6144:8192].rearrange("p (k h) -> p k h", k=KD)]
        woa_slab = [wkb1[:, 0:1024], wkb1[:, 1024:2048]]
        wob_vv = [wkb1[:, 0:2048].rearrange("p (c t) -> p c t", c=2),
                  wkb1[:, 2048:4096].rearrange("p (c t) -> p c t", c=2)]
        z2f = wkb1f[:, 2048:3072]
        slf = wkb1f[:, 3072:4096]
        xio = [xio0, xio1]

        C_SX, C_SQ, C_MEAN, C_T0, C_T1, C_T2, C_T3, C_NM = 0, 1, 2, 3, 4, 5, 6, 7
        C_VARX, C_RSTD, C_RSTD2 = 8, 16, 24
        C_M1, C_M2, C_P1, C_P2 = 32, 33, 34, 35
        C_NV, C_NH = 36, 37
        C_MU3, C_RSTD3, C_RMUL, C_NMU = 38, 39, 40, 41
        SC = lambda c: stats[:, c:c + 1]

        def EM(eng, fn, reads=(), writes=(), dma=False, sem=None):
            return P.emit(eng, fn, reads, writes, dma, sem)

        def DBG(name, src_ap, rbuf):
            if dbg and name in dbg_h:
                EM("sync", d(dbg_h[name], src_ap), reads=[rbuf], dma=True, sem="so")

        def d(dst, src):
            return lambda e, dst=dst, src=src: e.dma_start(out=dst, in_=src)
        for dst, src in [(ps_bc[:], ps_bc_h), (bk_p[:], bk_p_h), (bq_p[:], bq_p_h),
                         (bv_p[:], bv_p_h), (fg_p[:], fg_p_h), (fb_p[:], fb_p_h),
                         (eg_p[:], eg_p_h), (eb_p[:], eb_p_h), (b1_p[:], b1_p_h),
                         (b2_p[:], b2_p_h), (rw_p[:], rw_p_h), (cs[:], consts_h)]:
            EM("sync", d(dst, src), writes=[b_consts], dma=True)
        for dst, src in [(lng_bc[:], lng_bc_h), (lnb_bc[:], lnb_bc_h),
                         (fg_bc[:], fg_bc_h), (fb_bc[:], fb_bc_h),
                         (ident_b[:], ident_h), (ut2_b[:], ut2_h)]:
            EM("gpsimd", d(dst, src), writes=[b_consts], dma=True)
        EM("vector", lambda e: e.memset(ones_c[:], 1.0), writes=[b_consts])

        rowsrc = xio0[0:1, 0:1024]
        rowtmp = xio1[0:1, 0:1024]
        for src_h, hi, lo in [(bkp_r_h, bkp_hi, bkp_lo), (bqp_r_h, bqp_hi, bqp_lo),
                              (bo_r_h, bo_hi, bo_lo)]:
            EM("sync", d(rowsrc, src_h), writes=[b_xio[0]], dma=True)
            EM("vector", lambda e, hi=hi: e.tensor_copy(hi[:], rowsrc),
               reads=[b_xio[0]], writes=[b_consts])
            EM("vector", lambda e, hi=hi: e.tensor_copy(rowtmp, hi[:]),
               reads=[b_consts], writes=[b_xio[1]])
            EM("vector", lambda e: e.tensor_sub(rowtmp, rowsrc, rowtmp),
               reads=[b_xio[0], b_xio[1]], writes=[b_xio[1]])
            EM("vector", lambda e, lo=lo: e.tensor_copy(lo[:], rowtmp),
               reads=[b_xio[1]], writes=[b_consts])
        EM("sync", d(xio0[0:8, 0:1024], bof_r_h), writes=[b_xio[0]], dma=True)
        EM("vector", lambda e: e.tensor_copy(bof_bf[:], xio0[0:8, 0:1024]),
           reads=[b_xio[0]], writes=[b_consts])
        EM("vector", lambda e: e.tensor_copy(rw_bf[:], rw_p[:]),
           reads=[b_consts], writes=[b_consts])
        for k in range(KD):
            EM("vector", lambda e, k=k: e.tensor_scalar(
                out=rwg[:, k * E:(k + 1) * E], in0=rw_p[:, k * E:(k + 1) * E],
                scalar1=fg_p[:, k:k + 1], scalar2=None, op0=ALU.mult),
               reads=[b_consts], writes=[b_consts])
        EM("vector", lambda e: e.tensor_copy(fb_cb[:], fb_p[:]),
           reads=[b_consts], writes=[b_consts])
        def brw_mm(e):
            for k in range(KD):
                inst = e.matmul(ps1[0:1, 0:E], fb_cb[:, k:k + 1], rw_bf[:, k * E:(k + 1) * E],
                                start=(k == 0), stop=(k == KD - 1))
            return inst
        EM("tensor", brw_mm, reads=[b_consts], writes=[b_ps1])
        EM("vector", lambda e: e.tensor_copy(brw_row[:], ps1[0:1, 0:E]),
           reads=[b_ps1], writes=[b_consts])
        EM("vector", lambda e: e.tensor_copy(brw_rb[:], brw_row[:]),
           reads=[b_consts], writes=[b_consts])
        EM("tensor", lambda e: e.matmul(ps1[0:128, 0:E], ones_c[:], brw_rb[:],
                                        start=True, stop=True),
           reads=[b_consts], writes=[b_ps1])
        EM("vector", lambda e: e.tensor_copy(brw_bc[:], ps1[0:128, 0:E]),
           reads=[b_ps1], writes=[b_consts])

        def newton_rsqrt(dst_col, src_col, eps, extra_reads=()):
            v = SC(C_NV); h = SC(C_NH)
            rd = [b_stats] + list(extra_reads)
            EM("vector", lambda e: e.tensor_scalar(out=v, in0=src_col, scalar1=eps,
                                                   scalar2=None, op0=ALU.add),
               reads=rd, writes=[b_stats])
            iv = v.bitcast(I32); iy = dst_col.bitcast(I32)
            EM("vector", lambda e: e.tensor_scalar(out=iy, in0=iv, scalar1=1,
                                                   scalar2=None, op0=ALU.logical_shift_right),
               reads=[b_stats], writes=[b_stats])
            EM("vector", lambda e: e.tensor_scalar(out=iy, in0=iy, scalar1=-1, scalar2=MAGIC,
                                                   op0=ALU.mult, op1=ALU.add),
               reads=[b_stats], writes=[b_stats])
            for _ in range(3):
                EM("vector", lambda e: e.tensor_scalar(out=h, in0=dst_col, scalar1=dst_col,
                                                       scalar2=None, op0=ALU.mult),
                   reads=[b_stats], writes=[b_stats])
                EM("vector", lambda e: e.tensor_scalar(out=h, in0=h, scalar1=v,
                                                       scalar2=None, op0=ALU.mult),
                   reads=[b_stats], writes=[b_stats])
                EM("vector", lambda e: e.tensor_scalar(out=h, in0=h, scalar1=-0.5, scalar2=1.5,
                                                       op0=ALU.mult, op1=ALU.add),
                   reads=[b_stats], writes=[b_stats])
                EM("vector", lambda e: e.tensor_scalar(out=dst_col, in0=dst_col, scalar1=h,
                                                       scalar2=None, op0=ALU.mult),
                   reads=[b_stats], writes=[b_stats])

        def transpose8(src_ap, dst_ap, src_bufs, dst_buf, tiles=8):
            def tp(e):
                for k in range(tiles):
                    inst = e.transpose(pt[:, k * 128:(k + 1) * 128],
                                       src_ap[:, k * 128:(k + 1) * 128], ident_b[:])
                return inst
            EM("tensor", tp, reads=list(src_bufs) + [b_consts], writes=[b_pt])
            EM("vector", lambda e: e.tensor_copy(dst_ap, pt[:].rearrange(
                "p (k t) -> p k t", k=8)[:, 0:tiles, :]),
               reads=[b_pt], writes=[dst_buf])

        def phase_chain(src_slot, sin_dst, cos_dst, t_u, t_g):
            sp = SLOT[src_slot]; u = SLOT[t_u]; gg = SLOT[t_g]
            bs, bu, bg = b_slot[src_slot], b_slot[t_u], b_slot[t_g]
            for shift, dst in ((0.0, sin_dst), (0.25, cos_dst)):
                EM("vector", lambda e, sh=shift: e.tensor_scalar(
                    out=u, in0=sp, scalar1=INV2PI, scalar2=0.5 + sh,
                    op0=ALU.mult, op1=ALU.add), reads=[bs], writes=[bu])
                EM("vector", lambda e: e.tensor_scalar(
                    out=gg, in0=u, scalar1=1.0, scalar2=None, op0=ALU.is_ge),
                   reads=[bu], writes=[bg])
                EM("vector", lambda e: e.tensor_scalar(
                    out=u, in0=u, scalar1=0.0, scalar2=None, op0=ALU.is_lt),
                   reads=[bu], writes=[bu])
                EM("vector", lambda e: e.tensor_sub(gg, gg, u),
                   reads=[bu, bg], writes=[bg])
                EM("vector", lambda e: e.scalar_tensor_tensor(
                    out=u, in0=gg, scalar=-TWOPI, in1=sp, op0=ALU.mult, op1=ALU.add),
                   reads=[bg, bs], writes=[bu])
                dd = SLOT[dst]; bd = b_slot[dst]
                if shift == 0.0:
                    EM("scalar", lambda e, dd=dd: e.activation(out=dd, in_=u, func=AF.Sin),
                       reads=[bu], writes=[bd])
                else:
                    EM("scalar", lambda e, dd=dd: e.activation(out=dd, in_=u, func=AF.Sin,
                                                               bias=PI / 2),
                       reads=[bu, b_consts], writes=[bd])

        for sup in range(nsup):
            t0 = sup * T
            EM("gpsimd", d(wkb0_v[:], wk_h.rearrange("(k p) m -> p k m", p=128)),
               writes=[b_wkb0], dma=True)
            EM("gpsimd", d(wkb1_v[:], wq_h.rearrange("(k p) m -> p k m", p=128)),
               writes=[b_wkb1], dma=True)

            for tb in range(NTB):
                xb = xio[tb % 2]; bxb = b_xio[tb % 2]
                EM("sync", d(xb[:], x_h[t0 + tb * 128: t0 + (tb + 1) * 128, :]),
                   writes=[bxb], dma=True)
                EM("vector", lambda e, xb=xb: e.reduce_sum(SC(C_SX), xb[:], axis=AX.X),
                   reads=[bxb], writes=[b_stats])
                EM("scalar", lambda e, xb=xb: e.activation(
                    out=SLOT[5], in_=xb[:], func=AF.Square, accum_out=SC(C_SQ)),
                   reads=[bxb], writes=[b_slot[5], b_stats])
                EM("vector", lambda e: e.tensor_scalar(out=SC(C_MEAN), in0=SC(C_SX),
                                                       scalar1=1.0 / D, scalar2=None, op0=ALU.mult),
                   reads=[b_stats], writes=[b_stats])
                EM("vector", lambda e: e.tensor_scalar(out=SC(C_T0), in0=SC(C_MEAN), scalar1=SC(C_MEAN),
                                                       scalar2=None, op0=ALU.mult),
                   reads=[b_stats], writes=[b_stats])
                EM("vector", lambda e, tb=tb: e.scalar_tensor_tensor(
                    out=SC(C_VARX + tb), in0=SC(C_SQ), scalar=1.0 / D, in1=SC(C_T0),
                    op0=ALU.mult, op1=ALU.subtract),
                   reads=[b_stats], writes=[b_stats])
                newton_rsqrt(SC(C_RSTD + tb), SC(C_VARX + tb), 1e-5)
                EM("vector", lambda e: e.tensor_scalar(out=SC(C_T1), in0=SC(C_MEAN),
                                                       scalar1=-1.0, scalar2=None, op0=ALU.mult),
                   reads=[b_stats], writes=[b_stats])
                EM("vector", lambda e, tb=tb, xb=xb: e.tensor_scalar(
                    out=xn_v[:, tb, :], in0=xb[:], scalar1=SC(C_T1),
                    scalar2=SC(C_RSTD + tb), op0=ALU.add, op1=ALU.mult),
                   reads=[bxb, b_stats], writes=[b_xn_tok])
                EM("vector", lambda e, tb=tb: e.tensor_scalar(
                    out=SC(C_T2), in0=SC(C_RSTD + tb), scalar1=SC(C_RSTD + tb),
                    scalar2=None, op0=ALU.mult),
                   reads=[b_stats], writes=[b_stats])
                EM("vector", lambda e, tb=tb: e.tensor_scalar(
                    out=SC(C_T3), in0=SC(C_VARX + tb), scalar1=SC(C_T2),
                    scalar2=None, op0=ALU.mult),
                   reads=[b_stats], writes=[b_stats])
                newton_rsqrt(SC(C_RSTD2 + tb), SC(C_T3), 1e-5)
                transpose8(xn_v[:, tb, :],
                           fm_v[:, :, tb * 128:(tb + 1) * 128],
                           [b_xn_tok], b_fm)

            if dbg and sup == 0:
                DBG("d_xn", xn_tok[:], b_xn_tok)
                DBG("d_fm", fm[:], b_fm)
                DBG("d_stats", stats[:], b_stats)
            if 'A1' not in stages:
                continue
            def qkv(wbuf_v, bwbuf, bias_p, bbias, dstv, bdst):
                for m in range(KD):
                    pmm = pm1 if (m % 2 == 0) else pm2
                    bp = b_pm1 if (m % 2 == 0) else b_pm2
                    def grp(e, m=m, pmm=pmm, wbuf_v=wbuf_v):
                        for sub in range(2):
                            for k in range(KD):
                                inst = e.matmul(pmm[:, sub * 512:(sub + 1) * 512],
                                                wbuf_v[:, k, m * 128:(m + 1) * 128],
                                                fm_v[:, k, sub * 512:(sub + 1) * 512],
                                                start=(k == 0), stop=(k == KD - 1))
                        return inst
                    EM("tensor", grp, reads=[bwbuf, b_fm], writes=[bp])
                    EM("scalar", lambda e, m=m, pmm=pmm, dstv=dstv, bias_p=bias_p:
                       e.activation(out=dstv[:, m, :], in_=pmm[:], func=AF.Identity,
                                    bias=bias_p[:, m:m + 1]),
                       reads=[bp, bbias], writes=[bdst])
            qkv(wkb0_v, b_wkb0, bk_p, b_consts, kfm_v, b_kfm)
            qkv(wkb1_v, b_wkb1, bq_p, b_consts, qfm_v, b_qfm)
            EM("gpsimd", d(wkb0_v[:], wv_h.rearrange("(k p) m -> p k m", p=128)),
               writes=[b_wkb0], dma=True)
            for m in range(KD):
                pmm = pm1 if (m % 2 == 0) else pm2
                bp = b_pm1 if (m % 2 == 0) else b_pm2
                vf = vfm[:, (m % 2) * 1024:((m % 2) + 1) * 1024]
                bvf = b_vfm[m % 2]
                def grpv(e, m=m, pmm=pmm):
                    for sub in range(2):
                        for k in range(KD):
                            inst = e.matmul(pmm[:, sub * 512:(sub + 1) * 512],
                                            wkb0_v[:, k, m * 128:(m + 1) * 128],
                                            fm_v[:, k, sub * 512:(sub + 1) * 512],
                                            start=(k == 0), stop=(k == KD - 1))
                    return inst
                EM("tensor", grpv, reads=[b_wkb0, b_fm], writes=[bp])
                EM("scalar", lambda e, pmm=pmm, vf=vf, m=m: e.activation(
                    out=vf, in_=pmm[:], func=AF.Identity, bias=bv_p[:, m:m + 1]),
                   reads=[bp, b_consts], writes=[bvf])
                def tpv(e, vf=vf):
                    for tb in range(NTB):
                        inst = e.transpose(pt[:, tb * 128:(tb + 1) * 128],
                                           vf[:, tb * 128:(tb + 1) * 128], ident_b[:])
                    return inst
                EM("tensor", tpv, reads=[bvf, b_consts], writes=[b_pt])
                EM("vector", lambda e, m=m: e.tensor_copy(
                    vtok_v[:, :, m * 128:(m + 1) * 128],
                    pt[:].rearrange("p (tb t) -> p tb t", tb=8)),
                   reads=[b_pt], writes=[b_vx])
            EM("gpsimd", d(wkb0_v[:], wkp_h.rearrange("(k p) m -> p k m", p=128)),
               writes=[b_wkb0], dma=True)
            EM("gpsimd", d(wkb1_v[:], wqp_h.rearrange("(k p) m -> p k m", p=128)),
               writes=[b_wkb1], dma=True)
            for tb in range(NTB):
                def lgrp(e, tb=tb):
                    for k in range(KD):
                        inst = e.matmul(ps1[:, 0:E], fm_v[:, k, tb * 128:(tb + 1) * 128],
                                        rwg[:, k * E:(k + 1) * E],
                                        start=(k == 0), stop=(k == KD - 1))
                    return inst
                EM("tensor", lgrp, reads=[b_fm, b_consts], writes=[b_ps1])
                gsl = gates[:, tb * E:(tb + 1) * E]
                EM("vector", lambda e, tb=tb, gsl=gsl: e.scalar_tensor_tensor(
                    out=gsl, in0=ps1[:, 0:E], scalar=SC(C_RSTD2 + tb), in1=brw_bc[:],
                    op0=ALU.mult, op1=ALU.add),
                   reads=[b_ps1, b_stats, b_consts], writes=[b_gates])
                EM("vector", lambda e, gsl=gsl: e.reduce_max(SC(C_M1), gsl, axis=AX.X),
                   reads=[b_gates], writes=[b_stats])
                e1 = SLOT[0][:, 0:E]; l2 = SLOT[1][:, 0:E]; e2 = SLOT[2][:, 0:E]
                EM("vector", lambda e, gsl=gsl, e1=e1: e.tensor_scalar(
                    out=e1, in0=gsl, scalar1=SC(C_M1), scalar2=None, op0=ALU.is_equal),
                   reads=[b_gates, b_stats], writes=[b_slot[0]])
                EM("vector", lambda e, gsl=gsl, e1=e1, l2=l2: e.scalar_tensor_tensor(
                    out=l2, in0=e1, scalar=-1e30, in1=gsl, op0=ALU.mult, op1=ALU.add),
                   reads=[b_slot[0], b_gates], writes=[b_slot[1]])
                EM("vector", lambda e, l2=l2: e.reduce_max(SC(C_M2), l2, axis=AX.X),
                   reads=[b_slot[1]], writes=[b_stats])
                EM("vector", lambda e: e.tensor_scalar(out=SC(C_T0), in0=SC(C_M1),
                                                       scalar1=SC(C_M2), scalar2=None,
                                                       op0=ALU.subtract),
                   reads=[b_stats], writes=[b_stats])
                EM("scalar", lambda e: e.activation(out=SC(C_T1), in_=SC(C_T0),
                                                    func=AF.Tanh, scale=0.5),
                   reads=[b_stats, b_consts], writes=[b_stats])
                EM("vector", lambda e: e.tensor_scalar(out=SC(C_P1), in0=SC(C_T1),
                                                       scalar1=0.5, scalar2=0.5,
                                                       op0=ALU.mult, op1=ALU.add),
                   reads=[b_stats], writes=[b_stats])
                EM("vector", lambda e: e.tensor_scalar(out=SC(C_P2), in0=SC(C_P1),
                                                       scalar1=-1.0, scalar2=1.0,
                                                       op0=ALU.mult, op1=ALU.add),
                   reads=[b_stats], writes=[b_stats])
                EM("vector", lambda e, l2=l2, e2=e2: e.tensor_scalar(
                    out=e2, in0=l2, scalar1=SC(C_M2), scalar2=None, op0=ALU.is_equal),
                   reads=[b_slot[1], b_stats], writes=[b_slot[2]])
                EM("vector", lambda e, e1=e1: e.tensor_scalar(
                    out=e1, in0=e1, scalar1=SC(C_P1), scalar2=None, op0=ALU.mult),
                   reads=[b_slot[0], b_stats], writes=[b_slot[0]])
                EM("vector", lambda e, gsl=gsl, e1=e1, e2=e2: e.scalar_tensor_tensor(
                    out=gsl, in0=e2, scalar=SC(C_P2), in1=e1, op0=ALU.mult, op1=ALU.add),
                   reads=[b_slot[0], b_slot[2], b_stats], writes=[b_gates])

            if dbg and sup == 0:
                DBG("d_kfm", g2b[:, 0:KD * 1024], b_kfm)
                DBG("d_qfm", g2b[:, KD * 1024:2 * KD * 1024], b_qfm)
                DBG("d_vtok", vx[:], b_vx)
                DBG("d_gates", gates[:], b_gates)
            if 'A2' not in stages:
                continue
            for tb in range(NTB):
                tsl = slice(tb * 128, (tb + 1) * 128)
                def pgrp(e, tb=tb, src=kfm_v, wv_=wkb0_v, hi=bkp_hi, lo=bkp_lo, pmm=pm1):
                    for dh in range(2):
                        dsl = slice(dh * 512, (dh + 1) * 512)
                        inst = e.matmul(pmm[:, dsl], ones_c[:], hi[0:1, dsl],
                                        start=True, stop=False)
                        inst = e.matmul(pmm[:, dsl], ones_c[:], lo[0:1, dsl],
                                        start=False, stop=False)
                        for k in range(KD):
                            inst = e.matmul(pmm[:, dsl], src[:, k, tb * 128:(tb + 1) * 128],
                                            wv_[:, k, dsl],
                                            start=False, stop=(k == KD - 1))
                    return inst
                EM("tensor", lambda e, tb=tb: pgrp(e, tb, kfm_v, wkb0_v, bkp_hi, bkp_lo, pm1),
                   reads=[b_kfm, b_wkb0, b_consts], writes=[b_pm1])
                EM("scalar", lambda e: e.activation(out=SLOT[0], in_=pm1[:], func=AF.Tanh),
                   reads=[b_pm1], writes=[b_slot[0]])
                if dbg and sup == 0 and tb == 0:
                    DBG("d_tanhk", SLOT[0], b_slot[0])
                EM("tensor", lambda e, tb=tb: pgrp(e, tb, qfm_v, wkb1_v, bqp_hi, bqp_lo, pm2),
                   reads=[b_qfm, b_wkb1, b_consts], writes=[b_pm2])
                EM("scalar", lambda e: e.activation(out=SLOT[1], in_=pm2[:], func=AF.Tanh),
                   reads=[b_pm2], writes=[b_slot[1]])
                EM("vector", lambda e: e.tensor_mul(SLOT[0], SLOT[0], ps_bc[:]),
                   reads=[b_slot[0], b_consts], writes=[b_slot[0]])
                if dbg and sup == 0 and tb == 0:
                    DBG("d_kp", SLOT[0], b_slot[0])
                EM("vector", lambda e: e.tensor_mul(SLOT[1], SLOT[1], ps_bc[:]),
                   reads=[b_slot[1], b_consts], writes=[b_slot[1]])
                phase_chain(0, 4, 5, 2, 3)
                if dbg and sup == 0 and tb == 0:
                    DBG("d_sink", SLOT[4], b_slot[4])
                    DBG("d_cosk", SLOT[5], b_slot[5])
                brb = SLOTB[2][:, 0:1024]; bib = SLOTB[3][:, 0:1024]
                EM("vector", lambda e, tb=tb, brb=brb: e.tensor_mul(
                    brb, vtok_v[:, tb, :], SLOT[5]),
                   reads=[b_vx, b_slot[5]], writes=[b_slot[2]])
                EM("vector", lambda e, tb=tb, bib=bib: e.tensor_mul(
                    bib, vtok_v[:, tb, :], SLOT[4]),
                   reads=[b_vx, b_slot[4]], writes=[b_slot[3]])
                if dbg and sup == 0 and tb == 0:
                    DBG("d_br", brb, b_slot[2])
                spq = SLOT[1]
                for shift, dsti in ((0.0, 5), (0.25, 0)):
                    tu = 4
                    tg = 0
                    uu = SLOT[tu]; gg2 = SLOT[tg]
                    EM("vector", lambda e, sh=shift, uu=uu: e.tensor_scalar(
                        out=uu, in0=spq, scalar1=INV2PI, scalar2=0.5 + sh,
                        op0=ALU.mult, op1=ALU.add),
                       reads=[b_slot[1]], writes=[b_slot[tu]])
                    EM("vector", lambda e, uu=uu, gg2=gg2: e.tensor_scalar(
                        out=gg2, in0=uu, scalar1=1.0, scalar2=None, op0=ALU.is_ge),
                       reads=[b_slot[tu]], writes=[b_slot[tg]])
                    EM("vector", lambda e, uu=uu: e.tensor_scalar(
                        out=uu, in0=uu, scalar1=0.0, scalar2=None, op0=ALU.is_lt),
                       reads=[b_slot[tu]], writes=[b_slot[tu]])
                    EM("vector", lambda e, uu=uu, gg2=gg2: e.tensor_sub(gg2, gg2, uu),
                       reads=[b_slot[tu], b_slot[tg]], writes=[b_slot[tg]])
                    EM("vector", lambda e, uu=uu, gg2=gg2: e.scalar_tensor_tensor(
                        out=uu, in0=gg2, scalar=-TWOPI, in1=spq, op0=ALU.mult, op1=ALU.add),
                       reads=[b_slot[tg], b_slot[1]], writes=[b_slot[tu]])
                    dd = SLOT[dsti]
                    if shift == 0.0:
                        EM("scalar", lambda e, uu=uu, dd=dd: e.activation(
                            out=dd, in_=uu, func=AF.Sin),
                           reads=[b_slot[tu]], writes=[b_slot[dsti]])
                    else:
                        EM("scalar", lambda e, uu=uu, dd=dd: e.activation(
                            out=dd, in_=uu, func=AF.Sin, bias=PI / 2),
                           reads=[b_slot[tu], b_consts], writes=[b_slot[dsti]])
                def cgrp(e, src, pmm):
                    for dh in range(2):
                        dsl = slice(dh * 512, (dh + 1) * 512)
                        e.matmul(pmm[0:64, dsl], ut2_b[0:64, :], src[0:64, dsl],
                                 start=True, stop=True, tile_position=(0, 0))
                        inst = e.matmul(pmm[64:128, dsl], ut2_b[64:128, :], src[64:128, dsl],
                                        start=True, stop=True, tile_position=(64, 64))
                    return inst
                EM("tensor", lambda e, brb=brb: cgrp(e, brb, pm1),
                   reads=[b_slot[2], b_consts], writes=[b_pm1])
                EM("tensor", lambda e, bib=bib: cgrp(e, bib, pm2),
                   reads=[b_slot[3], b_consts], writes=[b_pm2])
                if dbg and sup == 0 and tb == 0:
                    DBG("d_sinq", SLOT[5], b_slot[5])
                    DBG("d_cosq", SLOT[0], b_slot[0])
                    EM("vector", lambda e: e.tensor_copy(htmp[:], pm1[:]),
                       reads=[b_pm1], writes=[b_htmp])
                    DBG("d_mr", htmp[:], b_htmp)
                EM("vector", lambda e: e.tensor_mul(SLOT[1], pm1[:], SLOT[0]),
                   reads=[b_pm1, b_slot[0]], writes=[b_slot[1]])
                EM("vector", lambda e: e.tensor_mul(SLOT[4], pm2[:], SLOT[5]),
                   reads=[b_pm2, b_slot[5]], writes=[b_slot[4]])
                EM("gpsimd", lambda e: e.tensor_add(SLOT[2], SLOT[1], SLOT[4]),
                   reads=[b_slot[1], b_slot[4]], writes=[b_slot[2]])
                if dbg and sup == 0 and tb == 0:
                    DBG("d_ret", SLOT[2], b_slot[2])
                EM("vector", lambda e: e.reduce_sum(SC(C_SX), SLOT[2], axis=AX.X),
                   reads=[b_slot[2]], writes=[b_stats])
                EM("scalar", lambda e: e.activation(out=SLOT[3], in_=SLOT[2],
                                                    func=AF.Square, accum_out=SC(C_SQ)),
                   reads=[b_slot[2]], writes=[b_slot[3], b_stats])
                EM("vector", lambda e: e.tensor_scalar(out=SC(C_MEAN), in0=SC(C_SX),
                                                       scalar1=1.0 / D, scalar2=None,
                                                       op0=ALU.mult),
                   reads=[b_stats], writes=[b_stats])
                EM("vector", lambda e: e.tensor_scalar(out=SC(C_T0), in0=SC(C_MEAN), scalar1=SC(C_MEAN),
                                                       scalar2=None, op0=ALU.mult),
                   reads=[b_stats], writes=[b_stats])
                EM("vector", lambda e: e.scalar_tensor_tensor(
                    out=SC(C_T3), in0=SC(C_SQ), scalar=1.0 / D, in1=SC(C_T0),
                    op0=ALU.mult, op1=ALU.subtract),
                   reads=[b_stats], writes=[b_stats])
                EM("vector", lambda e: e.tensor_scalar(out=SC(C_T3), in0=SC(C_T3),
                                                       scalar1=1.0 / D, scalar2=None,
                                                       op0=ALU.mult),
                   reads=[b_stats], writes=[b_stats])
                newton_rsqrt(SC(C_RMUL), SC(C_T3), 1e-5)
                EM("vector", lambda e: e.tensor_scalar(out=SC(C_RMUL), in0=SC(C_RMUL),
                                                       scalar1=1.0 / 32.0, scalar2=None,
                                                       op0=ALU.mult),
                   reads=[b_stats], writes=[b_stats])
                EM("vector", lambda e: e.tensor_scalar(out=SC(C_NMU), in0=SC(C_MEAN),
                                                       scalar1=-1.0, scalar2=None,
                                                       op0=ALU.mult),
                   reads=[b_stats], writes=[b_stats])
                EM("vector", lambda e: e.tensor_scalar(
                    out=SLOT[0], in0=SLOT[2], scalar1=SC(C_NMU), scalar2=SC(C_RMUL),
                    op0=ALU.add, op1=ALU.mult),
                   reads=[b_slot[2], b_stats], writes=[b_slot[0]])
                EM("gpsimd", lambda e: e.tensor_mul(SLOT[1], SLOT[0], lng_bc[:]),
                   reads=[b_slot[0], b_consts], writes=[b_slot[1]])
                nrmb = SLOTB[4][:, 0:1024]
                EM("gpsimd", lambda e, nrmb=nrmb: e.tensor_add(nrmb, SLOT[1], lnb_bc[:]),
                   reads=[b_slot[1], b_consts], writes=[b_slot[4]])
                if dbg and sup == 0 and tb == 0:
                    DBG("d_nrm", nrmb, b_slot[4])
                transpose8(nrmb, fm_v[:, :, tb * 128:(tb + 1) * 128], [b_slot[4]], b_fm)

            if 'A3' not in stages:
                continue
            for grp in range(2):
                tbs = list(range(grp * 4, grp * 4 + 4))
                pms = [(pm1, b_pm1), (pm2, b_pm2), (pm3, b_pm3), None]
                for k in range(KD):
                    par = k % 2
                    EM("gpsimd", d(woa_slab[par][:], woa_h[k * 128:(k + 1) * 128, :]),
                       writes=[b_wob[par]], dma=True)
                    for i, tb in enumerate(tbs):
                        def sgrp(e, k=k, i=i, tb=tb, par=par):
                            for dh in range(2):
                                dsl = slice(dh * 512, (dh + 1) * 512)
                                if i < 3:
                                    dst = pms[i][0][:, dsl]
                                else:
                                    dst = ps1[:, 0:512] if dh == 0 else pt32[:, 0:512]
                                if k == 0:
                                    e.matmul(dst, ones_c[:], bo_hi[0:1, dsl],
                                             start=True, stop=False)
                                    e.matmul(dst, ones_c[:], bo_lo[0:1, dsl],
                                             start=False, stop=False)
                                inst = e.matmul(dst, fm_v[:, k, tb * 128:(tb + 1) * 128],
                                                woa_slab[par][:, dsl],
                                                start=False, stop=(k == KD - 1))
                            return inst
                        wr = [pms[i][1]] if i < 3 else [b_ps1, b_pt]
                        EM("tensor", sgrp, reads=[b_fm, b_wob[par], b_consts], writes=wr)
                for i, tb in enumerate(tbs):
                    if i < 3:
                        EM("vector", lambda e, i=i, tb=tb: e.tensor_tensor(
                            out=s_v[:, tb, :], in0=pms[i][0][:], in1=xn_v[:, tb, :],
                            op=ALU.add),
                           reads=[pms[i][1], b_xn_tok], writes=[b_s])
                    else:
                        EM("vector", lambda e, tb=tb: e.tensor_tensor(
                            out=s_v[:, tb, 0:512], in0=ps1[:, 0:512],
                            in1=xn_v[:, tb, 0:512], op=ALU.add),
                           reads=[b_ps1, b_xn_tok], writes=[b_s])
                        EM("vector", lambda e, tb=tb: e.tensor_tensor(
                            out=s_v[:, tb, 512:1024], in0=pt32[:, 0:512],
                            in1=xn_v[:, tb, 512:1024], op=ALU.add),
                           reads=[b_pt, b_xn_tok], writes=[b_s])

            if dbg and sup == 0:
                DBG("d_s", s_sb[:], b_s)
            if 'F0' not in stages:
                continue
            for tb in range(NTB):
                EM("vector", lambda e, tb=tb: e.tensor_scalar(
                    out=SLOT[0], in0=xn_v[:, tb, :], scalar1=SC(C_RSTD2 + tb),
                    scalar2=None, op0=ALU.mult),
                   reads=[b_xn_tok, b_stats], writes=[b_slot[0]])
                EM("gpsimd", lambda e: e.tensor_mul(SLOT[1], SLOT[0], fg_bc[:]),
                   reads=[b_slot[0], b_consts], writes=[b_slot[1]])
                EM("gpsimd", lambda e: e.tensor_add(SLOT[0], SLOT[1], fb_bc[:]),
                   reads=[b_slot[1], b_consts], writes=[b_slot[0]])
                if dbg and sup == 0 and tb == 0:
                    DBG("d_xf", SLOT[0], b_slot[0])
                EM("vector", lambda e: e.reduce_sum(SC(C_SX), SLOT[0], axis=AX.X),
                   reads=[b_slot[0]], writes=[b_stats])
                EM("scalar", lambda e: e.activation(out=SLOT[1], in_=SLOT[0],
                                                    func=AF.Square, accum_out=SC(C_SQ)),
                   reads=[b_slot[0]], writes=[b_slot[1], b_stats])
                EM("vector", lambda e: e.tensor_scalar(out=SC(C_MU3), in0=SC(C_SX),
                                                       scalar1=1.0 / D, scalar2=None,
                                                       op0=ALU.mult),
                   reads=[b_stats], writes=[b_stats])
                EM("vector", lambda e: e.tensor_scalar(out=SC(C_T0), in0=SC(C_MU3), scalar1=SC(C_MU3),
                                                       scalar2=None, op0=ALU.mult),
                   reads=[b_stats], writes=[b_stats])
                EM("vector", lambda e: e.scalar_tensor_tensor(
                    out=SC(C_T3), in0=SC(C_SQ), scalar=1.0 / D, in1=SC(C_T0),
                    op0=ALU.mult, op1=ALU.subtract),
                   reads=[b_stats], writes=[b_stats])
                newton_rsqrt(SC(C_RSTD3), SC(C_T3), 1e-5)
                EM("vector", lambda e: e.tensor_scalar(out=SC(C_NMU), in0=SC(C_MU3),
                                                       scalar1=-1.0, scalar2=None,
                                                       op0=ALU.mult),
                   reads=[b_stats], writes=[b_stats])
                EM("vector", lambda e: e.tensor_scalar(
                    out=ytmp, in0=SLOT[0], scalar1=SC(C_NMU), scalar2=SC(C_RSTD3),
                    op0=ALU.add, op1=ALU.mult),
                   reads=[b_slot[0], b_stats], writes=[b_vx])
                transpose8(ytmp, fm_v[:, :, tb * 128:(tb + 1) * 128], [b_vx], b_fm)
                EM("gpsimd", lambda e, tb=tb: e.memset(acc_v[:, tb, :], 0.0),
                   writes=[b_acc])
                EM("vector", lambda e, tb=tb: e.tensor_copy(
                    gbf[:], gates[:, tb * E:(tb + 1) * E]),
                   reads=[b_gates], writes=[b_gbf])
                EM("tensor", lambda e: e.transpose(pt[0:E, 0:128], gbf[:], ident_b[:]),
                   reads=[b_gbf, b_consts], writes=[b_pt])
                EM("vector", lambda e, tb=tb: e.tensor_copy(
                    gatesT[0:E, tb * 128:(tb + 1) * 128], pt[0:E, 0:128]),
                   reads=[b_pt], writes=[b_gatesT])

            if dbg and sup == 0:
                DBG("d_yfm", fm[:], b_fm)
            if 'F1' not in stages:
                continue
            for ex in range(E):
                for k in range(KD):
                    EM("vector", lambda e, ex=ex, k=k: e.tensor_scalar(
                        out=xe_v[:, k, :], in0=fm_v[:, k, :],
                        scalar1=eg_p[:, ex * KD + k:ex * KD + k + 1],
                        scalar2=eb_p[:, ex * KD + k:ex * KD + k + 1],
                        op0=ALU.mult, op1=ALU.add),
                       reads=[b_fm, b_consts], writes=[b_vx])
                for q in range(4):
                    for hcp in range(4):
                        par = hcp % 2
                        c0 = (q * 8 + hcp * 2) * 128
                        EM("gpsimd", d(w1b_v[par][:],
                                       w1_h[ex, :, c0:c0 + 256].rearrange(
                                           "(k p) h -> p k h", p=128)),
                           writes=[b_w1b[par]], dma=True)
                        EM("gpsimd", d(w2b_v[par][:],
                                       w2_h[ex, :, c0:c0 + 256].rearrange(
                                           "(k p) h -> p k h", p=128)),
                           writes=[b_w2b[par]], dma=True)
                        for hcl in range(2):
                            hcq = hcp * 2 + hcl
                            hcg = q * 8 + hcq
                            p2 = hcq % 2
                            x1p, bx1 = (pm1, b_pm1) if p2 == 0 else (pm3, b_pm3)
                            if p2 == 0:
                                x2ps = [(pm2[:, 0:512], 0), (pm2[:, 512:1024], 512)]
                                bx2 = [b_pm2]
                            else:
                                x2ps = [(ps1[:, 0:512], 0), (pt32[:, 0:512], 512)]
                                bx2 = [b_ps1, b_pt]
                            zf = z2f if p2 == 0 else SLOT[4]
                            sf = slf if p2 == 0 else SLOT[5]
                            hf = htmp if p2 == 0 else htmp2
                            bz = b_z2 if p2 == 0 else b_z2o
                            bs = b_sl if p2 == 0 else b_slo
                            bh = b_htmp if p2 == 0 else b_htmp2
                            bzw = b_slot[4] if p2 == 1 else bz
                            bsw = b_slot[5] if p2 == 1 else bs
                            def x1g(e, par=par, hcl=hcl, x1p=x1p):
                                for sub in range(2):
                                    for k in range(KD):
                                        inst = e.matmul(
                                            x1p[:, sub * 512:(sub + 1) * 512],
                                            w1b_v[par][:, k, hcl * 128:(hcl + 1) * 128],
                                            xe_v[:, k, sub * 512:(sub + 1) * 512],
                                            start=(k == 0), stop=(k == KD - 1))
                                return inst
                            EM("tensor", x1g, reads=[b_w1b[par], b_vx], writes=[bx1])
                            def x2g(e, par=par, hcl=hcl, x2ps=x2ps):
                                for (dst, off) in x2ps:
                                    for k in range(KD):
                                        inst = e.matmul(
                                            dst,
                                            w2b_v[par][:, k, hcl * 128:(hcl + 1) * 128],
                                            xe_v[:, k, off // 512 * 512:
                                                 off // 512 * 512 + 512]
                                            if False else
                                            xe_v[:, k, off:off + 512],
                                            start=(k == 0), stop=(k == KD - 1))
                                return inst
                            EM("tensor", x2g, reads=[b_w2b[par], b_vx], writes=bx2)
                            if p2 == 0:
                                EM("scalar", lambda e, ex=ex, hcg=hcg, zf=zf: e.activation(
                                    out=zf, in_=pm2[:], func=AF.Identity,
                                    bias=b2_p[:, ex * 32 + hcg:ex * 32 + hcg + 1]),
                                   reads=[b_pm2, b_consts], writes=[bzw])
                            else:
                                EM("scalar", lambda e, ex=ex, hcg=hcg, zf=zf: e.activation(
                                    out=zf[:, 0:512], in_=ps1[:, 0:512], func=AF.Identity,
                                    bias=b2_p[:, ex * 32 + hcg:ex * 32 + hcg + 1]),
                                   reads=[b_ps1, b_consts], writes=[bzw])
                                EM("scalar", lambda e, ex=ex, hcg=hcg, zf=zf: e.activation(
                                    out=zf[:, 512:1024], in_=pt32[:, 0:512], func=AF.Identity,
                                    bias=b2_p[:, ex * 32 + hcg:ex * 32 + hcg + 1]),
                                   reads=[b_pt, b_consts], writes=[bzw])
                            EM("vector", lambda e, zf=zf, sf=sf: e.tensor_scalar(
                                out=sf, in0=zf, scalar1=-20.0, scalar2=20.0,
                                op0=ALU.max, op1=ALU.min),
                               reads=[bzw], writes=[bsw])
                            EM("scalar", lambda e, zf=zf, sf=sf: e.activation(
                                out=zf, in_=sf, func=AF.Silu),
                               reads=[bsw], writes=[bzw])
                            EM("vector", lambda e, ex=ex, hcg=hcg, x1p=x1p, zf=zf, hf=hf:
                               e.scalar_tensor_tensor(
                                   out=hf[:], in0=x1p[:],
                                   scalar=b1_p[:, ex * 32 + hcg:ex * 32 + hcg + 1],
                                   in1=zf, op0=ALU.add, op1=ALU.mult),
                               reads=[bx1, bzw, b_consts], writes=[bh])
                            EM("vector", lambda e, hcq=hcq, hf=hf: e.tensor_scalar(
                                out=h_v[:, hcq, :], in0=hf[:], scalar1=-1e4,
                                scalar2=1e4, op0=ALU.max, op1=ALU.min),
                               reads=[bh], writes=[b_h[hcq]])
                            if dbg and sup == 0 and ex == 0 and q == 0 and hcq == 0:
                                DBG("d_z2", z2f, b_z2)
                                DBG("d_h0", h_v[:, 0, :], b_h[0])
                    for grp in range(2):
                        tbs = list(range(grp * 4, grp * 4 + 4))
                        for hcp in range(4):
                            par = hcp % 2
                            r0 = (q * 8 + hcp * 2) * 128
                            EM("gpsimd", d(wob_vv[par][:],
                                           wof_h[ex, r0:r0 + 256, :].rearrange(
                                               "(c p) t -> p c t", p=128)),
                               writes=[b_wob[par]], dma=True)
                            for hcl in range(2):
                                hcq = hcp * 2 + hcl
                                for i, tb in enumerate(tbs):
                                    def og(e, par=par, hcl=hcl, hcq=hcq, i=i, tb=tb):
                                        st = (hcq == 0); sp = (hcq == 7)
                                        if i == 0:
                                            e.matmul(pm1[:, 0:512],
                                                     h_v[:, hcq, tb * 128:(tb + 1) * 128],
                                                     wob_vv[par][:, hcl, 0:512],
                                                     start=st, stop=sp)
                                            return e.matmul(pm1[:, 512:1024],
                                                            h_v[:, hcq, tb * 128:(tb + 1) * 128],
                                                            wob_vv[par][:, hcl, 512:1024],
                                                            start=st, stop=sp)
                                        if i == 1:
                                            e.matmul(pm2[:, 0:512],
                                                     h_v[:, hcq, tb * 128:(tb + 1) * 128],
                                                     wob_vv[par][:, hcl, 0:512],
                                                     start=st, stop=sp)
                                            return e.matmul(pm2[:, 512:1024],
                                                            h_v[:, hcq, tb * 128:(tb + 1) * 128],
                                                            wob_vv[par][:, hcl, 512:1024],
                                                            start=st, stop=sp)
                                        if i == 2:
                                            e.matmul(pm3[:, 0:512],
                                                     h_v[:, hcq, tb * 128:(tb + 1) * 128],
                                                     wob_vv[par][:, hcl, 0:512],
                                                     start=st, stop=sp)
                                            return e.matmul(pm3[:, 512:1024],
                                                            h_v[:, hcq, tb * 128:(tb + 1) * 128],
                                                            wob_vv[par][:, hcl, 512:1024],
                                                            start=st, stop=sp)
                                        e.matmul(ps1[:, 0:512],
                                                 h_v[:, hcq, tb * 128:(tb + 1) * 128],
                                                 wob_vv[par][:, hcl, 0:512],
                                                 start=st, stop=sp)
                                        return e.matmul(pt32[:, 0:512],
                                                        h_v[:, hcq, tb * 128:(tb + 1) * 128],
                                                        wob_vv[par][:, hcl, 512:1024],
                                                        start=st, stop=sp)
                                    wr = [[b_pm1], [b_pm2], [b_pm3], [b_ps1, b_pt]][i]
                                    EM("tensor", og,
                                       reads=[b_h[hcq], b_wob[par]], writes=wr)
                        for i, tb in enumerate(tbs):
                            gcol = gates[:, tb * E + ex:tb * E + ex + 1]
                            if i < 3:
                                pmm = [pm1, pm2, pm3][i]
                                bp = [b_pm1, b_pm2, b_pm3][i]
                                EM("vector", lambda e, tb=tb, pmm=pmm, gcol=gcol:
                                   e.scalar_tensor_tensor(
                                       out=acc_v[:, tb, :], in0=pmm[:], scalar=gcol,
                                       in1=acc_v[:, tb, :], op0=ALU.mult, op1=ALU.add),
                                   reads=[bp, b_gates, b_acc], writes=[b_acc])
                            else:
                                EM("vector", lambda e, tb=tb, gcol=gcol:
                                   e.scalar_tensor_tensor(
                                       out=acc_v[:, tb, 0:512], in0=ps1[:, 0:512],
                                       scalar=gcol, in1=acc_v[:, tb, 0:512],
                                       op0=ALU.mult, op1=ALU.add),
                                   reads=[b_ps1, b_gates, b_acc], writes=[b_acc])
                                EM("vector", lambda e, tb=tb, gcol=gcol:
                                   e.scalar_tensor_tensor(
                                       out=acc_v[:, tb, 512:1024], in0=pt32[:, 0:512],
                                       scalar=gcol, in1=acc_v[:, tb, 512:1024],
                                       op0=ALU.mult, op1=ALU.add),
                                   reads=[b_pt, b_gates, b_acc], writes=[b_acc])

            if dbg and sup == 0:
                DBG("d_acc", g2[:], b_acc)
            if 'F2' not in stages:
                continue
            for tb in range(NTB):
                def gbo(e, tb=tb):
                    e.matmul(pm3[:, 0:512], gatesT[0:E, tb * 128:(tb + 1) * 128],
                             bof_bf[0:E, 0:512], start=True, stop=True)
                    return e.matmul(pm3[:, 512:1024], gatesT[0:E, tb * 128:(tb + 1) * 128],
                                    bof_bf[0:E, 512:1024], start=True, stop=True)
                EM("tensor", gbo, reads=[b_gatesT, b_consts], writes=[b_pm3])
                EM("vector", lambda e, tb=tb: e.tensor_add(SLOT[0], acc_v[:, tb, :], pm3[:]),
                   reads=[b_acc, b_pm3], writes=[b_slot[0]])
                EM("scalar", lambda e: e.activation(out=SLOT[1], in_=SLOT[0], func=AF.Silu),
                   reads=[b_slot[0]], writes=[b_slot[1]])
                xb = xio[tb % 2]; bxb = b_xio[tb % 2]
                EM("sync", d(xb[:], x_h[t0 + tb * 128: t0 + (tb + 1) * 128, :]),
                   writes=[bxb], dma=True)
                EM("gpsimd", lambda e, tb=tb: e.tensor_mul(SLOT[2], s_v[:, tb, :], SLOT[1]),
                   reads=[b_s, b_slot[1]], writes=[b_slot[2]])
                EM("vector", lambda e, xb=xb: e.tensor_add(SLOT[3], SLOT[2], xb[:]),
                   reads=[b_slot[2], bxb], writes=[b_slot[3]])
                EM("sync", lambda e, t0=t0, tb=tb: e.dma_start(
                    out=y_h[t0 + tb * 128: t0 + (tb + 1) * 128, :], in_=SLOT[3]),
                   reads=[b_slot[3]], dma=True, sem="so")

        P.finalize()
        n_so = P.final_counts.get("so", 0)

        with nc.Block() as block:
            def mk(eng_name):
                ops = P.q[eng_name]
                def body(eng):
                    waited = {}
                    for op in ops:
                        for dpd in op.waits:
                            if waited.get(dpd.sem, 0) < dpd.val:
                                eng.wait_ge(sems[dpd.sem], dpd.val)
                                waited[dpd.sem] = dpd.val
                        inst = op.fn(eng)
                        if op.need_inc:
                            inst.then_inc(sems[op.sem], op.n)
                    if eng_name == "sync":
                        eng.wait_ge(sems["so"], n_so)
                return body
            block.sync(mk("sync"))
            block.gpsimd(mk("gpsimd"))
            block.scalar(mk("scalar"))
            block.vector(mk("vector"))
            block.tensor(mk("tensor"))

    return nc


_NC_CACHE = None


def kernel(**inputs):
    global _NC_CACHE
    ii = {k: np.asarray(v) for k, v in inputs.items()}

    def P(v):
        return np.ascontiguousarray(v, dtype=np.float32)

    com = {
        "wk": P(ii["L0_attn_Wk"]), "wv": P(ii["L0_attn_Wv"]), "wq": P(ii["L0_attn_Wq"]),
        "wkp": P(ii["L0_attn_Wkp"]), "wqp": P(ii["L0_attn_Wqp"]), "woa": P(ii["L0_attn_Wo"]),
        "w1": P(ii["L0_ffn_W1"]), "w2": P(ii["L0_ffn_W2"]), "wof": P(ii["L0_ffn_Wo"]),
        "rw_p": P(ii["L0_ffn_router_w"].reshape(8, 128, 8).transpose(1, 0, 2).reshape(128, 64)),
        "bk_p": P(ii["L0_attn_bk"].reshape(8, 128).T),
        "bq_p": P(ii["L0_attn_bq"].reshape(8, 128).T),
        "bv_p": P(ii["L0_attn_bv"].reshape(8, 128).T),
        "fg_p": P(ii["L0_ffn_ln_g"].reshape(8, 128).T),
        "fb_p": P(ii["L0_ffn_ln_b"].reshape(8, 128).T),
        "eg_p": P(ii["L0_ffn_eg"].reshape(8, 8, 128).transpose(2, 0, 1).reshape(128, 64)),
        "eb_p": P(ii["L0_ffn_eb"].reshape(8, 8, 128).transpose(2, 0, 1).reshape(128, 64)),
        "b1_p": P(ii["L0_ffn_b1"].reshape(8, 32, 128).transpose(2, 0, 1).reshape(128, 256)),
        "b2_p": P(ii["L0_ffn_b2"].reshape(8, 32, 128).transpose(2, 0, 1).reshape(128, 256)),
        "bkp_r": P(ii["L0_attn_bkp"].reshape(1, 1024)),
        "bqp_r": P(ii["L0_attn_bqp"].reshape(1, 1024)),
        "bo_r": P(ii["L0_attn_bo"].reshape(1, 1024)),
        "bof_r": P(ii["L0_ffn_bo"]),
        "ps_bc": P(np.broadcast_to(ii["L0_attn_phase_scale"], (128, 1024))),
        "lng_bc": P(np.broadcast_to(ii["L0_attn_ln_g"], (128, 1024))),
        "lnb_bc": P(np.broadcast_to(ii["L0_attn_ln_b"], (128, 1024))),
        "fg_bc": P(np.broadcast_to(ii["L0_ffn_ln_g"], (128, 1024))),
        "fb_bc": P(np.broadcast_to(ii["L0_ffn_ln_b"], (128, 1024))),
        "ident": np.eye(128, dtype=np.float32),
        "ut2": np.vstack([np.triu(np.ones((64, 64), np.float32))] * 2),
        "consts": P(np.broadcast_to(np.array([PI / 2, 0.5, 0.0, 0.0], np.float32), (128, 4))),
    }
    x = P(ii["L0_attn_x"])
    in_maps = [dict(com, x=x[c]) for c in range(8)]

    if _NC_CACHE is None:
        _NC_CACHE = build_nc()
    res = run_bass_kernel_spmd(_NC_CACHE, in_maps, core_ids=list(range(8)))
    out = np.stack([np.asarray(r["y"]) for r in res.results], axis=0)
    return out.astype(np.float32)


# version 05
# speedup vs baseline: 1.0227x; 1.0227x over previous
import numpy as np
import concourse.bass as bass
import concourse.mybir as mybir
from concourse.bass_utils import run_bass_kernel_spmd

F32 = mybir.dt.float32
BF16 = mybir.dt.bfloat16
I32 = mybir.dt.int32
AF = mybir.ActivationFunctionType
ALU = mybir.AluOpType
AX = mybir.AxisListType

D = 1024
HD = 4096
E = 8
S_CORE = 8192
T = 1024
NSUP = S_CORE // T
NTB = T // 128
KD = D // 128
PI = float(np.pi)
INV2PI = float(1.0 / (2 * np.pi))
TWOPI = float(2 * np.pi)
MAGIC = 0x5F3759DF

ENGINES = ("sync", "gpsimd", "scalar", "vector", "tensor")
ESEM = {"sync": "sd", "gpsimd": "gd", "scalar": "se", "vector": "ve", "tensor": "pe"}


class Buf:
    __slots__ = ("name", "ready", "readers")

    def __init__(self, name):
        self.name = name
        self.ready = None
        self.readers = []


class Op:
    __slots__ = ("eng", "fn", "waits", "sem", "n", "need_inc", "val", "async_done")

    def __init__(self, eng, fn, sem, n, async_done):
        self.eng = eng
        self.fn = fn
        self.waits = []
        self.sem = sem
        self.n = n
        self.need_inc = False
        self.val = None
        self.async_done = async_done


class Prog:
    def __init__(self):
        self.q = {e: [] for e in ENGINES}

    def emit(self, eng, fn, reads=(), writes=(), dma=False, sem=None):
        sem = sem or ESEM[eng]
        op = Op(eng, fn, sem, 16 if dma else 1, dma)
        deps = []
        for b in reads:
            if b.ready is not None:
                deps.append(b.ready)
        for w in writes:
            deps.extend(w.readers)
            if w.ready is not None:
                deps.append(w.ready)
        seen = set()
        for d in deps:
            if id(d) in seen:
                continue
            seen.add(id(d))
            op.waits.append(d)
        self.q[eng].append(op)
        for w in writes:
            w.ready = op
            w.readers = []
        for b in reads:
            b.readers.append(op)
        return op

    def finalize(self):
        for q in self.q.values():
            for op in q:
                if op.n == 16:
                    op.need_inc = True
                for d in op.waits:
                    d.need_inc = True
        cnt = {}
        order = []
        for e in ENGINES:
            order.extend(self.q[e])
        for e in ENGINES:
            for op in self.q[e]:
                if op.need_inc:
                    cnt[op.sem] = cnt.get(op.sem, 0) + op.n
                    op.val = cnt[op.sem]
        self.final_counts = cnt


def build_nc(dbg=False, stages=('X','A1','A2','A3','F0','F1','F2'), nsup=NSUP):
    nc = bass.Bass("TRN2", target_bir_lowering=False)

    dr = {}
    def din(name, shape, dtype=F32):
        dr[name] = nc.dram_tensor(name, list(shape), dtype, kind="ExternalInput").ap()
        return dr[name]

    x_h = din("x", (S_CORE, D))
    wk_h = din("wk", (D, D)); wv_h = din("wv", (D, D)); wq_h = din("wq", (D, D))
    wkp_h = din("wkp", (D, D)); wqp_h = din("wqp", (D, D)); woa_h = din("woa", (D, D))
    w1_h = din("w1", (E, D, HD)); w2_h = din("w2", (E, D, HD)); wof_h = din("wof", (E, HD, D))
    rw_p_h = din("rw_p", (128, KD * E))
    bk_p_h = din("bk_p", (128, KD)); bq_p_h = din("bq_p", (128, KD)); bv_p_h = din("bv_p", (128, KD))
    fg_p_h = din("fg_p", (128, KD)); fb_p_h = din("fb_p", (128, KD))
    eg_p_h = din("eg_p", (128, E * KD)); eb_p_h = din("eb_p", (128, E * KD))
    b1_p_h = din("b1_p", (128, E * 32)); b2_p_h = din("b2_p", (128, E * 32))
    bkp_r_h = din("bkp_r", (1, D)); bqp_r_h = din("bqp_r", (1, D)); bo_r_h = din("bo_r", (1, D))
    bof_r_h = din("bof_r", (E, D))
    ps_bc_h = din("ps_bc", (128, D))
    lng_bc_h = din("lng_bc", (128, D)); lnb_bc_h = din("lnb_bc", (128, D))
    fg_bc_h = din("fg_bc", (128, D)); fb_bc_h = din("fb_bc", (128, D))
    ident_h = din("ident", (128, 128))
    ut2_h = din("ut2", (128, 64))
    consts_h = din("consts", (128, 4))
    y_h = nc.dram_tensor("y", [S_CORE, D], F32, kind="ExternalOutput").ap()
    dbg_h = {}
    if dbg:
        for nm, sh, dt in [
            ("d_xn", (128, 8192), BF16), ("d_fm", (128, 8192), BF16),
            ("d_kfm", (128, 8192), BF16), ("d_qfm", (128, 8192), BF16),
            ("d_vtok", (128, 8192), BF16), ("d_gates", (128, 64), F32),
            ("d_stats", (128, 64), F32), ("d_s", (128, 8192), BF16),
            ("d_yfm", (128, 8192), BF16), ("d_acc", (128, 8192), F32),
            ("d_tanhk", (128, 1024), F32), ("d_kp", (128, 1024), F32),
            ("d_sink", (128, 1024), F32), ("d_cosk", (128, 1024), F32),
            ("d_br", (128, 1024), BF16), ("d_ret", (128, 1024), F32),
            ("d_nrm", (128, 1024), BF16), ("d_xf", (128, 1024), F32),
            ("d_h0", (128, 1024), BF16), ("d_z2", (128, 1024), F32),
            ("d_sinq", (128, 1024), F32), ("d_cosq", (128, 1024), F32),
            ("d_mr", (128, 1024), F32),
        ]:
            dbg_h[nm] = nc.dram_tensor(nm, list(sh), dt, kind="ExternalOutput").ap()

    from contextlib import ExitStack
    es = ExitStack()
    with es:
        sb = lambda name, shape, dt=F32: es.enter_context(nc.sbuf_tensor(name, list(shape), dt))
        ps = lambda name, shape, dt=F32: es.enter_context(nc.psum_tensor(name, list(shape), dt))

        g1 = sb("g1", (128, 6 * 1024))
        g1b = g1[:].bitcast(BF16)
        xn_tok = sb("xn_tok", (128, NTB * 1024), BF16)
        fm = sb("fm", (128, KD * 1024), BF16)
        g2 = sb("g2", (128, 8 * 1024))
        g2b = g2[:].bitcast(BF16)
        vx = sb("vx", (128, NTB * 1024), BF16)
        wkb0 = sb("wkb0", (128, KD * 1024), BF16)
        wkb1 = sb("wkb1", (128, KD * 1024), BF16)
        wkb1f = wkb1[:].bitcast(F32)
        s_sb = sb("s_sb", (128, NTB * 1024), BF16)
        xio0 = sb("xio0", (128, 1024)); xio1 = sb("xio1", (128, 1024))
        htmp = sb("htmp", (128, 1024))
        htmp2 = sb("htmp2", (128, 1024))
        vfm = sb("vfm", (128, 2 * 1024), BF16)
        stats = sb("stats", (128, 96))
        gates = sb("gates", (128, NTB * E))
        gatesT = sb("gatesT", (8, 1024), BF16)
        gbf = sb("gbf", (128, 8), BF16)
        bkp_hi = sb("bkp_hi", (1, D), BF16); bkp_lo = sb("bkp_lo", (1, D), BF16)
        bqp_hi = sb("bqp_hi", (1, D), BF16); bqp_lo = sb("bqp_lo", (1, D), BF16)
        bo_hi = sb("bo_hi", (1, D), BF16); bo_lo = sb("bo_lo", (1, D), BF16)
        bof_bf = sb("bof_bf", (E, D), BF16)
        ps_bc = sb("ps_bc_s", (128, D))
        lng_bc = sb("lng_bc_s", (128, D), BF16); lnb_bc = sb("lnb_bc_s", (128, D), BF16)
        fg_bc = sb("fg_bc_s", (128, D), BF16); fb_bc = sb("fb_bc_s", (128, D), BF16)
        bk_p = sb("bk_p_s", (128, KD)); bq_p = sb("bq_p_s", (128, KD)); bv_p = sb("bv_p_s", (128, KD))
        fg_p = sb("fg_p_s", (128, KD)); fb_p = sb("fb_p_s", (128, KD))
        fb_cb = sb("fb_cb", (128, KD), BF16)
        eg_p = sb("eg_p_s", (128, E * KD)); eb_p = sb("eb_p_s", (128, E * KD))
        b1_p = sb("b1_p_s", (128, E * 32)); b2_p = sb("b2_p_s", (128, E * 32))
        rw_p = sb("rw_p_s", (128, KD * E))
        rw_bf = sb("rw_bf", (128, KD * E), BF16)
        rwg = sb("rwg", (128, KD * E), BF16)
        brw_row = sb("brw_row", (1, E)); brw_rb = sb("brw_rb", (1, E), BF16)
        brw_bc = sb("brw_bc", (128, E))
        ident_b = sb("ident_b", (128, 128), BF16)
        ut2_b = sb("ut2_b", (128, 64), BF16)
        ones_c = sb("ones_c", (1, 128), BF16)
        cs = sb("cs", (128, 4))

        pm1 = ps("pm1", (128, 1024))
        pm2 = ps("pm2", (128, 1024))
        pm3 = ps("pm3", (128, 1024))
        ps1 = ps("ps1", (128, 512))
        pt = ps("pt", (128, 1024), BF16)
        pt32 = pt[:].bitcast(F32)

        sems = {}
        for s in ("sd", "gd", "se", "ve", "pe", "so"):
            sems[s] = es.enter_context(nc.semaphore(s))

        nc.const_aps.register(F32, PI / 2, cs[:, 0:1])
        nc.const_aps.register(F32, 0.5, cs[:, 1:2])

        P = Prog()
        B = lambda n: Buf(n)
        b_xn_tok = B("xn_tok"); b_fm = B("fm"); b_kfm = B("kfm"); b_qfm = B("qfm")
        b_vx = B("vx"); b_wkb0 = B("wkb0"); b_wkb1 = B("wkb1"); b_s = B("s")
        b_xio = [B("xio0"), B("xio1")]; b_htmp = B("htmp")
        b_vfm = [B("vfm0"), B("vfm1")]
        b_stP = B("stP"); b_stT = [B("stE"), B("stO")]
        b_gates = B("gates"); b_gatesT = B("gatesT"); b_gbf = B("gbf")
        b_acc = B("acc")
        b_slot = [B(f"S{i}") for i in range(6)]
        b_pm1 = B("pm1"); b_pm2 = B("pm2"); b_pm3 = B("pm3"); b_ps1 = B("ps1"); b_pt = B("pt")
        b_consts = B("consts")
        b_w1b = [B("w1b0"), B("w1b1")]; b_w2b = [B("w2b0"), B("w2b1")]
        b_wob = [B("wob0"), B("wob1")]
        b_z2 = B("z2"); b_sl = B("sl")
        b_z2o = B("z2o"); b_slo = B("slo"); b_htmp2 = B("htmp2")
        b_h = [B(f"h{i}") for i in range(8)]
        b_y = B("y_dummy")

        SLOT = [g1[:, i * 1024:(i + 1) * 1024] for i in range(6)]
        SLOTB = [g1b[:, i * 2048:(i + 1) * 2048] for i in range(6)]
        h_v = g1b[:].rearrange("p (hc t) -> p hc t", hc=12)
        xn_v = xn_tok[:].rearrange("p (tb t) -> p tb t", tb=NTB)
        fm_v = fm[:].rearrange("p (k t) -> p k t", k=KD)
        kfm_v = g2b[:, 0:KD * 1024].rearrange("p (k t) -> p k t", k=KD)
        qfm_v = g2b[:, KD * 1024:2 * KD * 1024].rearrange("p (k t) -> p k t", k=KD)
        acc_v = g2[:].rearrange("p (tb t) -> p tb t", tb=NTB)
        vtok_v = vx[:].rearrange("p (tb t) -> p tb t", tb=NTB)
        xe_v = vx[:].rearrange("p (k t) -> p k t", k=KD)
        ytmp = vx[:, 0:1024]
        s_v = s_sb[:].rearrange("p (tb t) -> p tb t", tb=NTB)
        wkb0_v = wkb0[:].rearrange("p (k t) -> p k t", k=KD)
        wkb1_v = wkb1[:].rearrange("p (k t) -> p k t", k=KD)
        w1b_v = [wkb0[:, 0:2048].rearrange("p (k h) -> p k h", k=KD),
                 wkb0[:, 2048:4096].rearrange("p (k h) -> p k h", k=KD)]
        w2b_v = [wkb0[:, 4096:6144].rearrange("p (k h) -> p k h", k=KD),
                 wkb0[:, 6144:8192].rearrange("p (k h) -> p k h", k=KD)]
        woa_slab = [wkb1[:, 0:1024], wkb1[:, 1024:2048]]
        wob_vv = [wkb1[:, 0:2048].rearrange("p (c t) -> p c t", c=2),
                  wkb1[:, 2048:4096].rearrange("p (c t) -> p c t", c=2)]
        z2f = wkb1f[:, 2048:3072]
        slf = wkb1f[:, 3072:4096]
        xio = [xio0, xio1]

        C_VARX, C_RSTD, C_RSTD2 = 8, 16, 24
        TB0 = {0: 32, 1: 64}
        SX, SQ, MEAN, T0, T1, T2, T3, NV, NH, M1, M2, P1, P2, MU3, RSTD3, NMU, RMUL = range(17)
        SC = lambda c: stats[:, c:c + 1]
        TC = lambda par, off: stats[:, TB0[par] + off:TB0[par] + off + 1]

        def EM(eng, fn, reads=(), writes=(), dma=False, sem=None):
            return P.emit(eng, fn, reads, writes, dma, sem)

        def DBG(name, src_ap, rbuf):
            if dbg and name in dbg_h:
                EM("sync", d(dbg_h[name], src_ap), reads=[rbuf], dma=True, sem="so")

        def d(dst, src):
            return lambda e, dst=dst, src=src: e.dma_start(out=dst, in_=src)
        for dst, src in [(ps_bc[:], ps_bc_h), (bk_p[:], bk_p_h), (bq_p[:], bq_p_h),
                         (bv_p[:], bv_p_h), (fg_p[:], fg_p_h), (fb_p[:], fb_p_h),
                         (eg_p[:], eg_p_h), (eb_p[:], eb_p_h), (b1_p[:], b1_p_h),
                         (b2_p[:], b2_p_h), (rw_p[:], rw_p_h), (cs[:], consts_h)]:
            EM("sync", d(dst, src), writes=[b_consts], dma=True)
        for dst, src in [(lng_bc[:], lng_bc_h), (lnb_bc[:], lnb_bc_h),
                         (fg_bc[:], fg_bc_h), (fb_bc[:], fb_bc_h),
                         (ident_b[:], ident_h), (ut2_b[:], ut2_h)]:
            EM("gpsimd", d(dst, src), writes=[b_consts], dma=True)
        EM("vector", lambda e: e.memset(ones_c[:], 1.0), writes=[b_consts])

        rowsrc = xio0[0:1, 0:1024]
        rowtmp = xio1[0:1, 0:1024]
        for src_h, hi, lo in [(bkp_r_h, bkp_hi, bkp_lo), (bqp_r_h, bqp_hi, bqp_lo),
                              (bo_r_h, bo_hi, bo_lo)]:
            EM("sync", d(rowsrc, src_h), writes=[b_xio[0]], dma=True)
            EM("vector", lambda e, hi=hi: e.tensor_copy(hi[:], rowsrc),
               reads=[b_xio[0]], writes=[b_consts])
            EM("vector", lambda e, hi=hi: e.tensor_copy(rowtmp, hi[:]),
               reads=[b_consts], writes=[b_xio[1]])
            EM("vector", lambda e: e.tensor_sub(rowtmp, rowsrc, rowtmp),
               reads=[b_xio[0], b_xio[1]], writes=[b_xio[1]])
            EM("vector", lambda e, lo=lo: e.tensor_copy(lo[:], rowtmp),
               reads=[b_xio[1]], writes=[b_consts])
        EM("sync", d(xio0[0:8, 0:1024], bof_r_h), writes=[b_xio[0]], dma=True)
        EM("vector", lambda e: e.tensor_copy(bof_bf[:], xio0[0:8, 0:1024]),
           reads=[b_xio[0]], writes=[b_consts])
        EM("vector", lambda e: e.tensor_copy(rw_bf[:], rw_p[:]),
           reads=[b_consts], writes=[b_consts])
        for k in range(KD):
            EM("vector", lambda e, k=k: e.tensor_scalar(
                out=rwg[:, k * E:(k + 1) * E], in0=rw_p[:, k * E:(k + 1) * E],
                scalar1=fg_p[:, k:k + 1], scalar2=None, op0=ALU.mult),
               reads=[b_consts], writes=[b_consts])
        EM("vector", lambda e: e.tensor_copy(fb_cb[:], fb_p[:]),
           reads=[b_consts], writes=[b_consts])
        def brw_mm(e):
            for k in range(KD):
                inst = e.matmul(ps1[0:1, 0:E], fb_cb[:, k:k + 1], rw_bf[:, k * E:(k + 1) * E],
                                start=(k == 0), stop=(k == KD - 1))
            return inst
        EM("tensor", brw_mm, reads=[b_consts], writes=[b_ps1])
        EM("vector", lambda e: e.tensor_copy(brw_row[:], ps1[0:1, 0:E]),
           reads=[b_ps1], writes=[b_consts])
        EM("vector", lambda e: e.tensor_copy(brw_rb[:], brw_row[:]),
           reads=[b_consts], writes=[b_consts])
        EM("tensor", lambda e: e.matmul(ps1[0:128, 0:E], ones_c[:], brw_rb[:],
                                        start=True, stop=True),
           reads=[b_consts], writes=[b_ps1])
        EM("vector", lambda e: e.tensor_copy(brw_bc[:], ps1[0:128, 0:E]),
           reads=[b_ps1], writes=[b_consts])

        def newton_rsqrt(dst_col, src_col, eps, par, extra_reads=(), b_stats=None):
            v = TC(par, NV); h = TC(par, NH)
            b_stats = b_stats or b_stT[par]
            rd = [b_stats] + list(extra_reads)
            EM("vector", lambda e: e.tensor_scalar(out=v, in0=src_col, scalar1=eps,
                                                   scalar2=None, op0=ALU.add),
               reads=rd, writes=[b_stats])
            iv = v.bitcast(I32); iy = dst_col.bitcast(I32)
            EM("vector", lambda e: e.tensor_scalar(out=iy, in0=iv, scalar1=1,
                                                   scalar2=None, op0=ALU.logical_shift_right),
               reads=[b_stats], writes=[b_stats])
            EM("vector", lambda e: e.tensor_scalar(out=iy, in0=iy, scalar1=-1, scalar2=MAGIC,
                                                   op0=ALU.mult, op1=ALU.add),
               reads=[b_stats], writes=[b_stats])
            for _ in range(3):
                EM("vector", lambda e: e.tensor_scalar(out=h, in0=dst_col, scalar1=dst_col,
                                                       scalar2=None, op0=ALU.mult),
                   reads=[b_stats], writes=[b_stats])
                EM("vector", lambda e: e.tensor_scalar(out=h, in0=h, scalar1=v,
                                                       scalar2=None, op0=ALU.mult),
                   reads=[b_stats], writes=[b_stats])
                EM("vector", lambda e: e.tensor_scalar(out=h, in0=h, scalar1=-0.5, scalar2=1.5,
                                                       op0=ALU.mult, op1=ALU.add),
                   reads=[b_stats], writes=[b_stats])
                EM("vector", lambda e: e.tensor_scalar(out=dst_col, in0=dst_col, scalar1=h,
                                                       scalar2=None, op0=ALU.mult),
                   reads=[b_stats], writes=[b_stats])

        def transpose8(src_ap, dst_ap, src_bufs, dst_buf, tiles=8):
            def tp(e):
                for k in range(tiles):
                    inst = e.transpose(pt[:, k * 128:(k + 1) * 128],
                                       src_ap[:, k * 128:(k + 1) * 128], ident_b[:])
                return inst
            EM("tensor", tp, reads=list(src_bufs) + [b_consts], writes=[b_pt])
            EM("vector", lambda e: e.tensor_copy(dst_ap, pt[:].rearrange(
                "p (k t) -> p k t", k=8)[:, 0:tiles, :]),
               reads=[b_pt], writes=[dst_buf])

        def phase_chain(src_slot, sin_dst, cos_dst, t_u, t_g):
            sp = SLOT[src_slot]; u = SLOT[t_u]; gg = SLOT[t_g]
            bs, bu, bg = b_slot[src_slot], b_slot[t_u], b_slot[t_g]
            for shift, dst in ((0.0, sin_dst), (0.25, cos_dst)):
                EM("vector", lambda e, sh=shift: e.tensor_scalar(
                    out=u, in0=sp, scalar1=INV2PI, scalar2=0.5 + sh,
                    op0=ALU.mult, op1=ALU.add), reads=[bs], writes=[bu])
                EM("vector", lambda e: e.tensor_scalar(
                    out=gg, in0=u, scalar1=1.0, scalar2=None, op0=ALU.is_ge),
                   reads=[bu], writes=[bg])
                EM("vector", lambda e: e.tensor_scalar(
                    out=u, in0=u, scalar1=0.0, scalar2=None, op0=ALU.is_lt),
                   reads=[bu], writes=[bu])
                EM("vector", lambda e: e.tensor_sub(gg, gg, u),
                   reads=[bu, bg], writes=[bg])
                EM("vector", lambda e: e.scalar_tensor_tensor(
                    out=u, in0=gg, scalar=-TWOPI, in1=sp, op0=ALU.mult, op1=ALU.add),
                   reads=[bg, bs], writes=[bu])
                dd = SLOT[dst]; bd = b_slot[dst]
                if shift == 0.0:
                    EM("scalar", lambda e, dd=dd: e.activation(out=dd, in_=u, func=AF.Sin),
                       reads=[bu], writes=[bd])
                else:
                    EM("scalar", lambda e, dd=dd: e.activation(out=dd, in_=u, func=AF.Sin,
                                                               bias=PI / 2),
                       reads=[bu, b_consts], writes=[bd])

        for sup in range(nsup):
            t0 = sup * T
            EM("gpsimd", d(wkb0_v[:], wk_h.rearrange("(k p) m -> p k m", p=128)),
               writes=[b_wkb0], dma=True)
            EM("gpsimd", d(wkb1_v[:], wq_h.rearrange("(k p) m -> p k m", p=128)),
               writes=[b_wkb1], dma=True)

            for tb in range(NTB):
                par = tb % 2
                xb = xio[par]; bxb = b_xio[par]
                bst = b_stT[par]
                junk = SLOT[5 - par]
                EM("sync", d(xb[:], x_h[t0 + tb * 128: t0 + (tb + 1) * 128, :]),
                   writes=[bxb], dma=True)
                EM("vector", lambda e, xb=xb, par=par: e.reduce_sum(TC(par, SX), xb[:], axis=AX.X),
                   reads=[bxb], writes=[bst])
                EM("scalar", lambda e, xb=xb, par=par, junk=junk: e.activation(
                    out=junk, in_=xb[:], func=AF.Square, accum_out=TC(par, SQ)),
                   reads=[bxb], writes=[b_slot[5 - par], bst])
                EM("vector", lambda e, par=par: e.tensor_scalar(out=TC(par, MEAN), in0=TC(par, SX),
                                                       scalar1=1.0 / D, scalar2=None, op0=ALU.mult),
                   reads=[bst], writes=[bst])
                EM("vector", lambda e, par=par: e.tensor_scalar(out=TC(par, T0), in0=TC(par, MEAN),
                                                       scalar1=TC(par, MEAN),
                                                       scalar2=None, op0=ALU.mult),
                   reads=[bst], writes=[bst])
                EM("vector", lambda e, tb=tb, par=par: e.scalar_tensor_tensor(
                    out=SC(C_VARX + tb), in0=TC(par, SQ), scalar=1.0 / D, in1=TC(par, T0),
                    op0=ALU.mult, op1=ALU.subtract),
                   reads=[bst], writes=[bst, b_stP])
                newton_rsqrt(SC(C_RSTD + tb), SC(C_VARX + tb), 1e-5, par,
                             extra_reads=[b_stP])
                EM("vector", lambda e, par=par: e.tensor_scalar(out=TC(par, T1), in0=TC(par, MEAN),
                                                       scalar1=-1.0, scalar2=None, op0=ALU.mult),
                   reads=[bst], writes=[bst])
                EM("vector", lambda e, tb=tb, xb=xb, par=par: e.tensor_scalar(
                    out=xn_v[:, tb, :], in0=xb[:], scalar1=TC(par, T1),
                    scalar2=SC(C_RSTD + tb), op0=ALU.add, op1=ALU.mult),
                   reads=[bxb, bst, b_stP], writes=[b_xn_tok])
                EM("vector", lambda e, tb=tb, par=par: e.tensor_scalar(
                    out=TC(par, T2), in0=SC(C_RSTD + tb), scalar1=SC(C_RSTD + tb),
                    scalar2=None, op0=ALU.mult),
                   reads=[bst, b_stP], writes=[bst])
                EM("vector", lambda e, tb=tb, par=par: e.tensor_scalar(
                    out=TC(par, T3), in0=SC(C_VARX + tb), scalar1=TC(par, T2),
                    scalar2=None, op0=ALU.mult),
                   reads=[bst, b_stP], writes=[bst])
                newton_rsqrt(TC(par, NMU), TC(par, T3), 1e-5, par)
                EM("vector", lambda e, tb=tb, par=par: e.tensor_scalar(
                    out=SC(C_RSTD2 + tb), in0=TC(par, NMU), scalar1=1.0,
                    scalar2=None, op0=ALU.mult),
                   reads=[bst], writes=[b_stP])
                transpose8(xn_v[:, tb, :],
                           fm_v[:, :, tb * 128:(tb + 1) * 128],
                           [b_xn_tok], b_fm)

            if dbg and sup == 0:
                DBG("d_xn", xn_tok[:], b_xn_tok)
                DBG("d_fm", fm[:], b_fm)
                DBG("d_stats", stats[:, 0:64], b_stP)
            if 'A1' not in stages:
                continue
            def qkv(wbuf_v, bwbuf, bias_p, bbias, dstv, bdst):
                for m in range(KD):
                    pmm = pm1 if (m % 2 == 0) else pm2
                    bp = b_pm1 if (m % 2 == 0) else b_pm2
                    def grp(e, m=m, pmm=pmm, wbuf_v=wbuf_v):
                        for sub in range(2):
                            for k in range(KD):
                                inst = e.matmul(pmm[:, sub * 512:(sub + 1) * 512],
                                                wbuf_v[:, k, m * 128:(m + 1) * 128],
                                                fm_v[:, k, sub * 512:(sub + 1) * 512],
                                                start=(k == 0), stop=(k == KD - 1))
                        return inst
                    EM("tensor", grp, reads=[bwbuf, b_fm], writes=[bp])
                    EM("scalar", lambda e, m=m, pmm=pmm, dstv=dstv, bias_p=bias_p:
                       e.activation(out=dstv[:, m, :], in_=pmm[:], func=AF.Identity,
                                    bias=bias_p[:, m:m + 1]),
                       reads=[bp, bbias], writes=[bdst])
            qkv(wkb0_v, b_wkb0, bk_p, b_consts, kfm_v, b_kfm)
            qkv(wkb1_v, b_wkb1, bq_p, b_consts, qfm_v, b_qfm)
            EM("gpsimd", d(wkb0_v[:], wv_h.rearrange("(k p) m -> p k m", p=128)),
               writes=[b_wkb0], dma=True)
            for m in range(KD):
                pmm = pm1 if (m % 2 == 0) else pm2
                bp = b_pm1 if (m % 2 == 0) else b_pm2
                vf = vfm[:, (m % 2) * 1024:((m % 2) + 1) * 1024]
                bvf = b_vfm[m % 2]
                def grpv(e, m=m, pmm=pmm):
                    for sub in range(2):
                        for k in range(KD):
                            inst = e.matmul(pmm[:, sub * 512:(sub + 1) * 512],
                                            wkb0_v[:, k, m * 128:(m + 1) * 128],
                                            fm_v[:, k, sub * 512:(sub + 1) * 512],
                                            start=(k == 0), stop=(k == KD - 1))
                    return inst
                EM("tensor", grpv, reads=[b_wkb0, b_fm], writes=[bp])
                EM("scalar", lambda e, pmm=pmm, vf=vf, m=m: e.activation(
                    out=vf, in_=pmm[:], func=AF.Identity, bias=bv_p[:, m:m + 1]),
                   reads=[bp, b_consts], writes=[bvf])
                def tpv(e, vf=vf):
                    for tb in range(NTB):
                        inst = e.transpose(pt[:, tb * 128:(tb + 1) * 128],
                                           vf[:, tb * 128:(tb + 1) * 128], ident_b[:])
                    return inst
                EM("tensor", tpv, reads=[bvf, b_consts], writes=[b_pt])
                EM("vector", lambda e, m=m: e.tensor_copy(
                    vtok_v[:, :, m * 128:(m + 1) * 128],
                    pt[:].rearrange("p (tb t) -> p tb t", tb=8)),
                   reads=[b_pt], writes=[b_vx])
            EM("gpsimd", d(wkb0_v[:], wkp_h.rearrange("(k p) m -> p k m", p=128)),
               writes=[b_wkb0], dma=True)
            EM("gpsimd", d(wkb1_v[:], wqp_h.rearrange("(k p) m -> p k m", p=128)),
               writes=[b_wkb1], dma=True)
            for tb in range(NTB):
                par = tb % 2
                bst = b_stT[par]
                def lgrp(e, tb=tb):
                    for k in range(KD):
                        inst = e.matmul(ps1[:, 0:E], fm_v[:, k, tb * 128:(tb + 1) * 128],
                                        rwg[:, k * E:(k + 1) * E],
                                        start=(k == 0), stop=(k == KD - 1))
                    return inst
                EM("tensor", lgrp, reads=[b_fm, b_consts], writes=[b_ps1])
                gsl = gates[:, tb * E:(tb + 1) * E]
                EM("vector", lambda e, tb=tb, gsl=gsl: e.scalar_tensor_tensor(
                    out=gsl, in0=ps1[:, 0:E], scalar=SC(C_RSTD2 + tb), in1=brw_bc[:],
                    op0=ALU.mult, op1=ALU.add),
                   reads=[b_ps1, b_stP, b_consts], writes=[b_gates])
                s0, s1, s2 = (0, 1, 2) if par == 0 else (3, 4, 5)
                e1 = SLOT[s0][:, 0:E]; l2 = SLOT[s1][:, 0:E]; e2 = SLOT[s2][:, 0:E]
                EM("vector", lambda e, gsl=gsl, par=par: e.reduce_max(TC(par, M1), gsl, axis=AX.X),
                   reads=[b_gates], writes=[bst])
                EM("vector", lambda e, gsl=gsl, e1=e1, par=par: e.tensor_scalar(
                    out=e1, in0=gsl, scalar1=TC(par, M1), scalar2=None, op0=ALU.is_equal),
                   reads=[b_gates, bst], writes=[b_slot[s0]])
                EM("vector", lambda e, gsl=gsl, e1=e1, l2=l2: e.scalar_tensor_tensor(
                    out=l2, in0=e1, scalar=-1e30, in1=gsl, op0=ALU.mult, op1=ALU.add),
                   reads=[b_slot[s0], b_gates], writes=[b_slot[s1]])
                EM("vector", lambda e, l2=l2, par=par: e.reduce_max(TC(par, M2), l2, axis=AX.X),
                   reads=[b_slot[s1]], writes=[bst])
                EM("vector", lambda e, par=par: e.tensor_scalar(out=TC(par, T0), in0=TC(par, M1),
                                                       scalar1=TC(par, M2), scalar2=None,
                                                       op0=ALU.subtract),
                   reads=[bst], writes=[bst])
                EM("scalar", lambda e, par=par: e.activation(out=TC(par, T1), in_=TC(par, T0),
                                                    func=AF.Tanh, scale=0.5),
                   reads=[bst, b_consts], writes=[bst])
                EM("vector", lambda e, par=par: e.tensor_scalar(out=TC(par, P1), in0=TC(par, T1),
                                                       scalar1=0.5, scalar2=0.5,
                                                       op0=ALU.mult, op1=ALU.add),
                   reads=[bst], writes=[bst])
                EM("vector", lambda e, par=par: e.tensor_scalar(out=TC(par, P2), in0=TC(par, P1),
                                                       scalar1=-1.0, scalar2=1.0,
                                                       op0=ALU.mult, op1=ALU.add),
                   reads=[bst], writes=[bst])
                EM("vector", lambda e, l2=l2, e2=e2, par=par: e.tensor_scalar(
                    out=e2, in0=l2, scalar1=TC(par, M2), scalar2=None, op0=ALU.is_equal),
                   reads=[b_slot[s1], bst], writes=[b_slot[s2]])
                EM("vector", lambda e, e1=e1, par=par: e.tensor_scalar(
                    out=e1, in0=e1, scalar1=TC(par, P1), scalar2=None, op0=ALU.mult),
                   reads=[b_slot[s0], bst], writes=[b_slot[s0]])
                EM("vector", lambda e, gsl=gsl, e1=e1, e2=e2, par=par: e.scalar_tensor_tensor(
                    out=gsl, in0=e2, scalar=TC(par, P2), in1=e1, op0=ALU.mult, op1=ALU.add),
                   reads=[b_slot[s0], b_slot[s2], bst], writes=[b_gates])

            if dbg and sup == 0:
                DBG("d_kfm", g2b[:, 0:KD * 1024], b_kfm)
                DBG("d_qfm", g2b[:, KD * 1024:2 * KD * 1024], b_qfm)
                DBG("d_vtok", vx[:], b_vx)
                DBG("d_gates", gates[:], b_gates)
            if 'A2' not in stages:
                continue
            PIH = float(np.pi / 2); PI3H = float(3 * np.pi / 2)

            def a2_phase(src_slot, sin_dst, cos_dst, ta, tg2):
                sp = SLOT[src_slot]; bs = b_slot[src_slot]
                for shift, dsti in ((0.0, sin_dst), (1.0, cos_dst)):
                    hi = PI if shift == 0.0 else PIH
                    lo = -PI if shift == 0.0 else -PI3H
                    u = SLOT[ta]; g2 = SLOT[tg2]
                    bu, bg = b_slot[ta], b_slot[tg2]
                    EM("vector", lambda e, u=u, hi=hi: e.tensor_scalar(
                        out=u, in0=sp, scalar1=hi, scalar2=None, op0=ALU.is_ge),
                       reads=[bs], writes=[bu])
                    EM("vector", lambda e, g2=g2, lo=lo: e.tensor_scalar(
                        out=g2, in0=sp, scalar1=lo, scalar2=None, op0=ALU.is_lt),
                       reads=[bs], writes=[bg])
                    EM("gpsimd", lambda e, u=u, g2=g2: e.tensor_sub(u, u, g2),
                       reads=[bu, bg], writes=[bu])
                    EM("vector", lambda e, u=u, g2=g2: e.scalar_tensor_tensor(
                        out=g2, in0=u, scalar=-TWOPI, in1=sp, op0=ALU.mult, op1=ALU.add),
                       reads=[bu, bs], writes=[bg])
                    dd = SLOT[dsti]
                    if shift == 0.0:
                        EM("scalar", lambda e, dd=dd, g2=g2: e.activation(
                            out=dd, in_=g2, func=AF.Sin),
                           reads=[bg], writes=[b_slot[dsti]])
                    else:
                        EM("scalar", lambda e, dd=dd, g2=g2: e.activation(
                            out=dd, in_=g2, func=AF.Sin, bias=PI / 2),
                           reads=[bg, b_consts], writes=[b_slot[dsti]])

            def a2_transpose(tb):
                nb = vfm[:, (tb % 2) * 1024:((tb % 2) + 1) * 1024]
                ptb = pt if tb % 2 == 0 else pm3[:].bitcast(BF16)[:, 0:1024]
                bptb = b_pt if tb % 2 == 0 else b_pm3
                def tp(e, nb=nb, ptb=ptb):
                    for k in range(8):
                        inst = e.transpose(ptb[:, k * 128:(k + 1) * 128],
                                           nb[:, k * 128:(k + 1) * 128], ident_b[:])
                    return inst
                EM("tensor", tp, reads=[b_vfm[tb % 2], b_consts], writes=[bptb])
                EM("vector", lambda e, tb=tb, ptb=ptb: e.tensor_copy(
                    fm_v[:, :, tb * 128:(tb + 1) * 128],
                    ptb.rearrange("p (k t) -> p k t", k=8)),
                   reads=[bptb], writes=[b_fm])

            for tb in range(NTB):
                par = tb % 2
                bst = b_stT[par]
                def pgrp(e, tb=tb, src=kfm_v, wv_=wkb0_v, hi=bkp_hi, lo=bkp_lo, pmm=pm1):
                    for dh in range(2):
                        dsl = slice(dh * 512, (dh + 1) * 512)
                        inst = e.matmul(pmm[:, dsl], ones_c[:], hi[0:1, dsl],
                                        start=True, stop=False)
                        inst = e.matmul(pmm[:, dsl], ones_c[:], lo[0:1, dsl],
                                        start=False, stop=False)
                        for k in range(KD):
                            inst = e.matmul(pmm[:, dsl], src[:, k, tb * 128:(tb + 1) * 128],
                                            wv_[:, k, dsl],
                                            start=False, stop=(k == KD - 1))
                    return inst
                EM("tensor", lambda e, tb=tb: pgrp(e, tb, kfm_v, wkb0_v, bkp_hi, bkp_lo, pm1),
                   reads=[b_kfm, b_wkb0, b_consts], writes=[b_pm1])
                EM("scalar", lambda e: e.activation(out=SLOT[0], in_=pm1[:], func=AF.Tanh),
                   reads=[b_pm1], writes=[b_slot[0]])
                if dbg and sup == 0 and tb == 0:
                    DBG("d_tanhk", SLOT[0], b_slot[0])
                EM("tensor", lambda e, tb=tb: pgrp(e, tb, qfm_v, wkb1_v, bqp_hi, bqp_lo, pm2),
                   reads=[b_qfm, b_wkb1, b_consts], writes=[b_pm2])
                EM("scalar", lambda e: e.activation(out=SLOT[1], in_=pm2[:], func=AF.Tanh),
                   reads=[b_pm2], writes=[b_slot[1]])
                EM("vector", lambda e: e.tensor_mul(SLOT[0], SLOT[0], ps_bc[:]),
                   reads=[b_slot[0], b_consts], writes=[b_slot[0]])
                if dbg and sup == 0 and tb == 0:
                    DBG("d_kp", SLOT[0], b_slot[0])
                EM("vector", lambda e: e.tensor_mul(SLOT[1], SLOT[1], ps_bc[:]),
                   reads=[b_slot[1], b_consts], writes=[b_slot[1]])
                a2_phase(0, 4, 5, 2, 3)
                if dbg and sup == 0 and tb == 0:
                    DBG("d_sink", SLOT[4], b_slot[4])
                    DBG("d_cosk", SLOT[5], b_slot[5])
                brb = SLOTB[2][:, 0:1024]; bib = SLOTB[3][:, 0:1024]
                EM("vector", lambda e, tb=tb, brb=brb: e.tensor_mul(
                    brb, vtok_v[:, tb, :], SLOT[5]),
                   reads=[b_vx, b_slot[5]], writes=[b_slot[2]])
                EM("vector", lambda e, tb=tb, bib=bib: e.tensor_mul(
                    bib, vtok_v[:, tb, :], SLOT[4]),
                   reads=[b_vx, b_slot[4]], writes=[b_slot[3]])
                if dbg and sup == 0 and tb == 0:
                    DBG("d_br", brb, b_slot[2])
                def cgrp(e, src2, pmm):
                    for dh in range(2):
                        dsl = slice(dh * 512, (dh + 1) * 512)
                        e.matmul(pmm[0:64, dsl], ut2_b[0:64, :], src2[0:64, dsl],
                                 start=True, stop=True, tile_position=(0, 0))
                        inst = e.matmul(pmm[64:128, dsl], ut2_b[64:128, :], src2[64:128, dsl],
                                        start=True, stop=True, tile_position=(64, 64))
                    return inst
                EM("tensor", lambda e, brb=brb: cgrp(e, brb, pm1),
                   reads=[b_slot[2], b_consts], writes=[b_pm1])
                EM("tensor", lambda e, bib=bib: cgrp(e, bib, pm2),
                   reads=[b_slot[3], b_consts], writes=[b_pm2])
                a2_phase(1, 0, 4, 2, 3)
                if tb > 0:
                    a2_transpose(tb - 1)
                if dbg and sup == 0 and tb == 0:
                    DBG("d_sinq", SLOT[0], b_slot[0])
                    DBG("d_cosq", SLOT[4], b_slot[4])
                    EM("vector", lambda e: e.tensor_copy(htmp[:], pm1[:]),
                       reads=[b_pm1], writes=[b_htmp])
                    DBG("d_mr", htmp[:], b_htmp)
                EM("vector", lambda e: e.tensor_mul(SLOT[1], pm1[:], SLOT[4]),
                   reads=[b_pm1, b_slot[4]], writes=[b_slot[1]])
                EM("vector", lambda e: e.tensor_mul(SLOT[2], pm2[:], SLOT[0]),
                   reads=[b_pm2, b_slot[0]], writes=[b_slot[2]])
                EM("gpsimd", lambda e: e.tensor_add(SLOT[3], SLOT[1], SLOT[2]),
                   reads=[b_slot[1], b_slot[2]], writes=[b_slot[3]])
                if dbg and sup == 0 and tb == 0:
                    DBG("d_ret", SLOT[3], b_slot[3])
                EM("vector", lambda e, par=par: e.reduce_sum(TC(par, SX), SLOT[3], axis=AX.X),
                   reads=[b_slot[3]], writes=[bst])
                EM("scalar", lambda e, par=par: e.activation(out=SLOT[1], in_=SLOT[3],
                                                    func=AF.Square, accum_out=TC(par, SQ)),
                   reads=[b_slot[3]], writes=[b_slot[1], bst])
                EM("vector", lambda e, par=par: e.tensor_scalar(out=TC(par, MEAN), in0=TC(par, SX),
                                                       scalar1=1.0 / D, scalar2=None,
                                                       op0=ALU.mult),
                   reads=[bst], writes=[bst])
                EM("vector", lambda e, par=par: e.tensor_scalar(out=TC(par, T0), in0=TC(par, MEAN),
                                                       scalar1=TC(par, MEAN), scalar2=None,
                                                       op0=ALU.mult),
                   reads=[bst], writes=[bst])
                EM("vector", lambda e, par=par: e.scalar_tensor_tensor(
                    out=TC(par, T3), in0=TC(par, SQ), scalar=1.0 / D, in1=TC(par, T0),
                    op0=ALU.mult, op1=ALU.subtract),
                   reads=[bst], writes=[bst])
                EM("vector", lambda e, par=par: e.tensor_scalar(out=TC(par, T3), in0=TC(par, T3),
                                                       scalar1=1.0 / D, scalar2=None,
                                                       op0=ALU.mult),
                   reads=[bst], writes=[bst])
                newton_rsqrt(TC(par, RMUL), TC(par, T3), 1e-5, par)
                EM("vector", lambda e, par=par: e.tensor_scalar(out=TC(par, RMUL), in0=TC(par, RMUL),
                                                       scalar1=1.0 / 32.0, scalar2=None,
                                                       op0=ALU.mult),
                   reads=[bst], writes=[bst])
                EM("vector", lambda e, par=par: e.tensor_scalar(out=TC(par, NMU), in0=TC(par, MEAN),
                                                       scalar1=-1.0, scalar2=None,
                                                       op0=ALU.mult),
                   reads=[bst], writes=[bst])
                EM("vector", lambda e, par=par: e.tensor_scalar(
                    out=SLOT[0], in0=SLOT[3], scalar1=TC(par, NMU), scalar2=TC(par, RMUL),
                    op0=ALU.add, op1=ALU.mult),
                   reads=[b_slot[3], bst], writes=[b_slot[0]])
                EM("gpsimd", lambda e: e.tensor_mul(SLOT[1], SLOT[0], lng_bc[:]),
                   reads=[b_slot[0], b_consts], writes=[b_slot[1]])
                nrmb = vfm[:, par * 1024:(par + 1) * 1024]
                EM("gpsimd", lambda e, nrmb=nrmb: e.tensor_add(nrmb, SLOT[1], lnb_bc[:]),
                   reads=[b_slot[1], b_consts], writes=[b_vfm[par]])
                if dbg and sup == 0 and tb == 0:
                    DBG("d_nrm", nrmb, b_vfm[par])
            a2_transpose(NTB - 1)

            if 'A3' not in stages:
                continue
            for grp in range(2):
                tbs = list(range(grp * 4, grp * 4 + 4))
                pms = [(pm1, b_pm1), (pm2, b_pm2), (pm3, b_pm3), None]
                for k in range(KD):
                    par = k % 2
                    EM("gpsimd", d(woa_slab[par][:], woa_h[k * 128:(k + 1) * 128, :]),
                       writes=[b_wob[par]], dma=True)
                    for i, tb in enumerate(tbs):
                        def sgrp(e, k=k, i=i, tb=tb, par=par):
                            for dh in range(2):
                                dsl = slice(dh * 512, (dh + 1) * 512)
                                if i < 3:
                                    dst = pms[i][0][:, dsl]
                                else:
                                    dst = ps1[:, 0:512] if dh == 0 else pt32[:, 0:512]
                                if k == 0:
                                    e.matmul(dst, ones_c[:], bo_hi[0:1, dsl],
                                             start=True, stop=False)
                                    e.matmul(dst, ones_c[:], bo_lo[0:1, dsl],
                                             start=False, stop=False)
                                inst = e.matmul(dst, fm_v[:, k, tb * 128:(tb + 1) * 128],
                                                woa_slab[par][:, dsl],
                                                start=False, stop=(k == KD - 1))
                            return inst
                        wr = [pms[i][1]] if i < 3 else [b_ps1, b_pt]
                        EM("tensor", sgrp, reads=[b_fm, b_wob[par], b_consts], writes=wr)
                for i, tb in enumerate(tbs):
                    if i < 3:
                        EM("vector", lambda e, i=i, tb=tb: e.tensor_tensor(
                            out=s_v[:, tb, :], in0=pms[i][0][:], in1=xn_v[:, tb, :],
                            op=ALU.add),
                           reads=[pms[i][1], b_xn_tok], writes=[b_s])
                    else:
                        EM("vector", lambda e, tb=tb: e.tensor_tensor(
                            out=s_v[:, tb, 0:512], in0=ps1[:, 0:512],
                            in1=xn_v[:, tb, 0:512], op=ALU.add),
                           reads=[b_ps1, b_xn_tok], writes=[b_s])
                        EM("vector", lambda e, tb=tb: e.tensor_tensor(
                            out=s_v[:, tb, 512:1024], in0=pt32[:, 0:512],
                            in1=xn_v[:, tb, 512:1024], op=ALU.add),
                           reads=[b_pt, b_xn_tok], writes=[b_s])

            if dbg and sup == 0:
                DBG("d_s", s_sb[:], b_s)
            if 'F0' not in stages:
                continue
            for tb in range(NTB):
                par = tb % 2
                bst = b_stT[par]
                sx0, sx1 = (0, 1) if par == 0 else (2, 3)
                xfs = SLOT[sx0]; tmp1 = SLOT[sx1]
                bxf, bt1 = b_slot[sx0], b_slot[sx1]
                ytp = vx[:, par * 1024:(par + 1) * 1024]
                EM("vector", lambda e, tb=tb, xfs=xfs, par=par: e.tensor_scalar(
                    out=xfs, in0=xn_v[:, tb, :], scalar1=SC(C_RSTD2 + tb),
                    scalar2=None, op0=ALU.mult),
                   reads=[b_xn_tok, b_stP], writes=[bxf])
                EM("gpsimd", lambda e, xfs=xfs, tmp1=tmp1: e.tensor_mul(tmp1, xfs, fg_bc[:]),
                   reads=[bxf, b_consts], writes=[bt1])
                EM("gpsimd", lambda e, xfs=xfs, tmp1=tmp1: e.tensor_add(xfs, tmp1, fb_bc[:]),
                   reads=[bt1, b_consts], writes=[bxf])
                if dbg and sup == 0 and tb == 0:
                    DBG("d_xf", xfs, bxf)
                EM("vector", lambda e, xfs=xfs, par=par: e.reduce_sum(TC(par, SX), xfs, axis=AX.X),
                   reads=[bxf], writes=[bst])
                EM("scalar", lambda e, xfs=xfs, tmp1=tmp1, par=par: e.activation(
                    out=tmp1, in_=xfs, func=AF.Square, accum_out=TC(par, SQ)),
                   reads=[bxf], writes=[bt1, bst])
                EM("vector", lambda e, par=par: e.tensor_scalar(out=TC(par, MU3), in0=TC(par, SX),
                                                       scalar1=1.0 / D, scalar2=None,
                                                       op0=ALU.mult),
                   reads=[bst], writes=[bst])
                EM("vector", lambda e, par=par: e.tensor_scalar(out=TC(par, T0), in0=TC(par, MU3),
                                                       scalar1=TC(par, MU3),
                                                       scalar2=None, op0=ALU.mult),
                   reads=[bst], writes=[bst])
                EM("vector", lambda e, par=par: e.scalar_tensor_tensor(
                    out=TC(par, T3), in0=TC(par, SQ), scalar=1.0 / D, in1=TC(par, T0),
                    op0=ALU.mult, op1=ALU.subtract),
                   reads=[bst], writes=[bst])
                newton_rsqrt(TC(par, RSTD3), TC(par, T3), 1e-5, par)
                EM("vector", lambda e, par=par: e.tensor_scalar(out=TC(par, NMU), in0=TC(par, MU3),
                                                       scalar1=-1.0, scalar2=None,
                                                       op0=ALU.mult),
                   reads=[bst], writes=[bst])
                EM("vector", lambda e, xfs=xfs, ytp=ytp, par=par: e.tensor_scalar(
                    out=ytp, in0=xfs, scalar1=TC(par, NMU), scalar2=TC(par, RSTD3),
                    op0=ALU.add, op1=ALU.mult),
                   reads=[bxf, bst], writes=[b_vfm[par]])
                ptb = pt if par == 0 else pm3[:].bitcast(BF16)[:, 0:1024]
                bptb = b_pt if par == 0 else b_pm3
                def tpy(e, ytp=ytp, ptb=ptb):
                    for k in range(8):
                        inst = e.transpose(ptb[:, k * 128:(k + 1) * 128],
                                           ytp[:, k * 128:(k + 1) * 128], ident_b[:])
                    return inst
                EM("tensor", tpy, reads=[b_vfm[par], b_consts], writes=[bptb])
                EM("vector", lambda e, tb=tb, ptb=ptb: e.tensor_copy(
                    fm_v[:, :, tb * 128:(tb + 1) * 128],
                    ptb.rearrange("p (k t) -> p k t", k=8)),
                   reads=[bptb], writes=[b_fm])
                EM("gpsimd", lambda e, tb=tb: e.memset(acc_v[:, tb, :], 0.0),
                   writes=[b_acc])
                gbs = gbf[:, 0:E]
                EM("vector", lambda e, tb=tb: e.tensor_copy(
                    gbf[:], gates[:, tb * E:(tb + 1) * E]),
                   reads=[b_gates], writes=[b_gbf])
                EM("tensor", lambda e: e.transpose(pt[0:E, 0:128], gbf[:], ident_b[:]),
                   reads=[b_gbf, b_consts], writes=[b_pt])
                EM("vector", lambda e, tb=tb: e.tensor_copy(
                    gatesT[0:E, tb * 128:(tb + 1) * 128], pt[0:E, 0:128]),
                   reads=[b_pt], writes=[b_gatesT])

            if dbg and sup == 0:
                DBG("d_yfm", fm[:], b_fm)
            if 'F1' not in stages:
                continue
            for ex in range(E):
                for k in range(KD):
                    EM("vector", lambda e, ex=ex, k=k: e.tensor_scalar(
                        out=xe_v[:, k, :], in0=fm_v[:, k, :],
                        scalar1=eg_p[:, ex * KD + k:ex * KD + k + 1],
                        scalar2=eb_p[:, ex * KD + k:ex * KD + k + 1],
                        op0=ALU.mult, op1=ALU.add),
                       reads=[b_fm, b_consts], writes=[b_vx])
                for q in range(4):
                    for hcp in range(4):
                        par = hcp % 2
                        c0 = (q * 8 + hcp * 2) * 128
                        EM("gpsimd", d(w1b_v[par][:],
                                       w1_h[ex, :, c0:c0 + 256].rearrange(
                                           "(k p) h -> p k h", p=128)),
                           writes=[b_w1b[par]], dma=True)
                        EM("gpsimd", d(w2b_v[par][:],
                                       w2_h[ex, :, c0:c0 + 256].rearrange(
                                           "(k p) h -> p k h", p=128)),
                           writes=[b_w2b[par]], dma=True)
                        for hcl in range(2):
                            hcq = hcp * 2 + hcl
                            hcg = q * 8 + hcq
                            p2 = hcq % 2
                            x1p, bx1 = (pm1, b_pm1) if p2 == 0 else (pm3, b_pm3)
                            if p2 == 0:
                                x2ps = [(pm2[:, 0:512], 0), (pm2[:, 512:1024], 512)]
                                bx2 = [b_pm2]
                            else:
                                x2ps = [(ps1[:, 0:512], 0), (pt32[:, 0:512], 512)]
                                bx2 = [b_ps1, b_pt]
                            zf = z2f if p2 == 0 else SLOT[4]
                            sf = slf if p2 == 0 else SLOT[5]
                            hf = htmp if p2 == 0 else htmp2
                            bz = b_z2 if p2 == 0 else b_z2o
                            bs = b_sl if p2 == 0 else b_slo
                            bh = b_htmp if p2 == 0 else b_htmp2
                            bzw = b_slot[4] if p2 == 1 else bz
                            bsw = b_slot[5] if p2 == 1 else bs
                            def x1g(e, par=par, hcl=hcl, x1p=x1p):
                                for sub in range(2):
                                    for k in range(KD):
                                        inst = e.matmul(
                                            x1p[:, sub * 512:(sub + 1) * 512],
                                            w1b_v[par][:, k, hcl * 128:(hcl + 1) * 128],
                                            xe_v[:, k, sub * 512:(sub + 1) * 512],
                                            start=(k == 0), stop=(k == KD - 1))
                                return inst
                            EM("tensor", x1g, reads=[b_w1b[par], b_vx], writes=[bx1])
                            def x2g(e, par=par, hcl=hcl, x2ps=x2ps):
                                for (dst, off) in x2ps:
                                    for k in range(KD):
                                        inst = e.matmul(
                                            dst,
                                            w2b_v[par][:, k, hcl * 128:(hcl + 1) * 128],
                                            xe_v[:, k, off // 512 * 512:
                                                 off // 512 * 512 + 512]
                                            if False else
                                            xe_v[:, k, off:off + 512],
                                            start=(k == 0), stop=(k == KD - 1))
                                return inst
                            EM("tensor", x2g, reads=[b_w2b[par], b_vx], writes=bx2)
                            if p2 == 0:
                                EM("scalar", lambda e, ex=ex, hcg=hcg, zf=zf: e.activation(
                                    out=zf, in_=pm2[:], func=AF.Identity,
                                    bias=b2_p[:, ex * 32 + hcg:ex * 32 + hcg + 1]),
                                   reads=[b_pm2, b_consts], writes=[bzw])
                            else:
                                EM("scalar", lambda e, ex=ex, hcg=hcg, zf=zf: e.activation(
                                    out=zf[:, 0:512], in_=ps1[:, 0:512], func=AF.Identity,
                                    bias=b2_p[:, ex * 32 + hcg:ex * 32 + hcg + 1]),
                                   reads=[b_ps1, b_consts], writes=[bzw])
                                EM("scalar", lambda e, ex=ex, hcg=hcg, zf=zf: e.activation(
                                    out=zf[:, 512:1024], in_=pt32[:, 0:512], func=AF.Identity,
                                    bias=b2_p[:, ex * 32 + hcg:ex * 32 + hcg + 1]),
                                   reads=[b_pt, b_consts], writes=[bzw])
                            EM("vector", lambda e, zf=zf, sf=sf: e.tensor_scalar(
                                out=sf, in0=zf, scalar1=-20.0, scalar2=20.0,
                                op0=ALU.max, op1=ALU.min),
                               reads=[bzw], writes=[bsw])
                            EM("scalar", lambda e, zf=zf, sf=sf: e.activation(
                                out=zf, in_=sf, func=AF.Silu),
                               reads=[bsw], writes=[bzw])
                            EM("vector", lambda e, ex=ex, hcg=hcg, x1p=x1p, zf=zf, hf=hf:
                               e.scalar_tensor_tensor(
                                   out=hf[:], in0=x1p[:],
                                   scalar=b1_p[:, ex * 32 + hcg:ex * 32 + hcg + 1],
                                   in1=zf, op0=ALU.add, op1=ALU.mult),
                               reads=[bx1, bzw, b_consts], writes=[bh])
                            EM("vector", lambda e, hcq=hcq, hf=hf: e.tensor_scalar(
                                out=h_v[:, hcq, :], in0=hf[:], scalar1=-1e4,
                                scalar2=1e4, op0=ALU.max, op1=ALU.min),
                               reads=[bh], writes=[b_h[hcq]])
                            if dbg and sup == 0 and ex == 0 and q == 0 and hcq == 0:
                                DBG("d_z2", z2f, b_z2)
                                DBG("d_h0", h_v[:, 0, :], b_h[0])
                    for grp in range(2):
                        tbs = list(range(grp * 4, grp * 4 + 4))
                        for hcp in range(4):
                            par = hcp % 2
                            r0 = (q * 8 + hcp * 2) * 128
                            EM("gpsimd", d(wob_vv[par][:],
                                           wof_h[ex, r0:r0 + 256, :].rearrange(
                                               "(c p) t -> p c t", p=128)),
                               writes=[b_wob[par]], dma=True)
                            for hcl in range(2):
                                hcq = hcp * 2 + hcl
                                for i, tb in enumerate(tbs):
                                    def og(e, par=par, hcl=hcl, hcq=hcq, i=i, tb=tb):
                                        st = (hcq == 0); sp = (hcq == 7)
                                        if i == 0:
                                            e.matmul(pm1[:, 0:512],
                                                     h_v[:, hcq, tb * 128:(tb + 1) * 128],
                                                     wob_vv[par][:, hcl, 0:512],
                                                     start=st, stop=sp)
                                            return e.matmul(pm1[:, 512:1024],
                                                            h_v[:, hcq, tb * 128:(tb + 1) * 128],
                                                            wob_vv[par][:, hcl, 512:1024],
                                                            start=st, stop=sp)
                                        if i == 1:
                                            e.matmul(pm2[:, 0:512],
                                                     h_v[:, hcq, tb * 128:(tb + 1) * 128],
                                                     wob_vv[par][:, hcl, 0:512],
                                                     start=st, stop=sp)
                                            return e.matmul(pm2[:, 512:1024],
                                                            h_v[:, hcq, tb * 128:(tb + 1) * 128],
                                                            wob_vv[par][:, hcl, 512:1024],
                                                            start=st, stop=sp)
                                        if i == 2:
                                            e.matmul(pm3[:, 0:512],
                                                     h_v[:, hcq, tb * 128:(tb + 1) * 128],
                                                     wob_vv[par][:, hcl, 0:512],
                                                     start=st, stop=sp)
                                            return e.matmul(pm3[:, 512:1024],
                                                            h_v[:, hcq, tb * 128:(tb + 1) * 128],
                                                            wob_vv[par][:, hcl, 512:1024],
                                                            start=st, stop=sp)
                                        e.matmul(ps1[:, 0:512],
                                                 h_v[:, hcq, tb * 128:(tb + 1) * 128],
                                                 wob_vv[par][:, hcl, 0:512],
                                                 start=st, stop=sp)
                                        return e.matmul(pt32[:, 0:512],
                                                        h_v[:, hcq, tb * 128:(tb + 1) * 128],
                                                        wob_vv[par][:, hcl, 512:1024],
                                                        start=st, stop=sp)
                                    wr = [[b_pm1], [b_pm2], [b_pm3], [b_ps1, b_pt]][i]
                                    EM("tensor", og,
                                       reads=[b_h[hcq], b_wob[par]], writes=wr)
                        for i, tb in enumerate(tbs):
                            gcol = gates[:, tb * E + ex:tb * E + ex + 1]
                            if i < 3:
                                pmm = [pm1, pm2, pm3][i]
                                bp = [b_pm1, b_pm2, b_pm3][i]
                                EM("vector", lambda e, tb=tb, pmm=pmm, gcol=gcol:
                                   e.scalar_tensor_tensor(
                                       out=acc_v[:, tb, :], in0=pmm[:], scalar=gcol,
                                       in1=acc_v[:, tb, :], op0=ALU.mult, op1=ALU.add),
                                   reads=[bp, b_gates, b_acc], writes=[b_acc])
                            else:
                                EM("vector", lambda e, tb=tb, gcol=gcol:
                                   e.scalar_tensor_tensor(
                                       out=acc_v[:, tb, 0:512], in0=ps1[:, 0:512],
                                       scalar=gcol, in1=acc_v[:, tb, 0:512],
                                       op0=ALU.mult, op1=ALU.add),
                                   reads=[b_ps1, b_gates, b_acc], writes=[b_acc])
                                EM("vector", lambda e, tb=tb, gcol=gcol:
                                   e.scalar_tensor_tensor(
                                       out=acc_v[:, tb, 512:1024], in0=pt32[:, 0:512],
                                       scalar=gcol, in1=acc_v[:, tb, 512:1024],
                                       op0=ALU.mult, op1=ALU.add),
                                   reads=[b_pt, b_gates, b_acc], writes=[b_acc])

            if dbg and sup == 0:
                DBG("d_acc", g2[:], b_acc)
            if 'F2' not in stages:
                continue
            for tb in range(NTB):
                def gbo(e, tb=tb):
                    e.matmul(pm3[:, 0:512], gatesT[0:E, tb * 128:(tb + 1) * 128],
                             bof_bf[0:E, 0:512], start=True, stop=True)
                    return e.matmul(pm3[:, 512:1024], gatesT[0:E, tb * 128:(tb + 1) * 128],
                                    bof_bf[0:E, 512:1024], start=True, stop=True)
                EM("tensor", gbo, reads=[b_gatesT, b_consts], writes=[b_pm3])
                EM("vector", lambda e, tb=tb: e.tensor_add(SLOT[0], acc_v[:, tb, :], pm3[:]),
                   reads=[b_acc, b_pm3], writes=[b_slot[0]])
                EM("scalar", lambda e: e.activation(out=SLOT[1], in_=SLOT[0], func=AF.Silu),
                   reads=[b_slot[0]], writes=[b_slot[1]])
                xb = xio[tb % 2]; bxb = b_xio[tb % 2]
                EM("sync", d(xb[:], x_h[t0 + tb * 128: t0 + (tb + 1) * 128, :]),
                   writes=[bxb], dma=True)
                EM("gpsimd", lambda e, tb=tb: e.tensor_mul(SLOT[2], s_v[:, tb, :], SLOT[1]),
                   reads=[b_s, b_slot[1]], writes=[b_slot[2]])
                EM("vector", lambda e, xb=xb: e.tensor_add(SLOT[3], SLOT[2], xb[:]),
                   reads=[b_slot[2], bxb], writes=[b_slot[3]])
                EM("sync", lambda e, t0=t0, tb=tb: e.dma_start(
                    out=y_h[t0 + tb * 128: t0 + (tb + 1) * 128, :], in_=SLOT[3]),
                   reads=[b_slot[3]], dma=True, sem="so")

        P.finalize()
        n_so = P.final_counts.get("so", 0)

        with nc.Block() as block:
            def mk(eng_name):
                ops = P.q[eng_name]
                def body(eng):
                    waited = {}
                    for op in ops:
                        for dpd in op.waits:
                            if waited.get(dpd.sem, 0) < dpd.val:
                                eng.wait_ge(sems[dpd.sem], dpd.val)
                                waited[dpd.sem] = dpd.val
                        inst = op.fn(eng)
                        if op.need_inc:
                            inst.then_inc(sems[op.sem], op.n)
                    if eng_name == "sync":
                        eng.wait_ge(sems["so"], n_so)
                return body
            block.sync(mk("sync"))
            block.gpsimd(mk("gpsimd"))
            block.scalar(mk("scalar"))
            block.vector(mk("vector"))
            block.tensor(mk("tensor"))

    return nc


_NC_CACHE = None


def kernel(**inputs):
    global _NC_CACHE
    ii = {k: np.asarray(v) for k, v in inputs.items()}

    def P(v):
        return np.ascontiguousarray(v, dtype=np.float32)

    com = {
        "wk": P(ii["L0_attn_Wk"]), "wv": P(ii["L0_attn_Wv"]), "wq": P(ii["L0_attn_Wq"]),
        "wkp": P(ii["L0_attn_Wkp"]), "wqp": P(ii["L0_attn_Wqp"]), "woa": P(ii["L0_attn_Wo"]),
        "w1": P(ii["L0_ffn_W1"]), "w2": P(ii["L0_ffn_W2"]), "wof": P(ii["L0_ffn_Wo"]),
        "rw_p": P(ii["L0_ffn_router_w"].reshape(8, 128, 8).transpose(1, 0, 2).reshape(128, 64)),
        "bk_p": P(ii["L0_attn_bk"].reshape(8, 128).T),
        "bq_p": P(ii["L0_attn_bq"].reshape(8, 128).T),
        "bv_p": P(ii["L0_attn_bv"].reshape(8, 128).T),
        "fg_p": P(ii["L0_ffn_ln_g"].reshape(8, 128).T),
        "fb_p": P(ii["L0_ffn_ln_b"].reshape(8, 128).T),
        "eg_p": P(ii["L0_ffn_eg"].reshape(8, 8, 128).transpose(2, 0, 1).reshape(128, 64)),
        "eb_p": P(ii["L0_ffn_eb"].reshape(8, 8, 128).transpose(2, 0, 1).reshape(128, 64)),
        "b1_p": P(ii["L0_ffn_b1"].reshape(8, 32, 128).transpose(2, 0, 1).reshape(128, 256)),
        "b2_p": P(ii["L0_ffn_b2"].reshape(8, 32, 128).transpose(2, 0, 1).reshape(128, 256)),
        "bkp_r": P(ii["L0_attn_bkp"].reshape(1, 1024)),
        "bqp_r": P(ii["L0_attn_bqp"].reshape(1, 1024)),
        "bo_r": P(ii["L0_attn_bo"].reshape(1, 1024)),
        "bof_r": P(ii["L0_ffn_bo"]),
        "ps_bc": P(np.broadcast_to(ii["L0_attn_phase_scale"], (128, 1024))),
        "lng_bc": P(np.broadcast_to(ii["L0_attn_ln_g"], (128, 1024))),
        "lnb_bc": P(np.broadcast_to(ii["L0_attn_ln_b"], (128, 1024))),
        "fg_bc": P(np.broadcast_to(ii["L0_ffn_ln_g"], (128, 1024))),
        "fb_bc": P(np.broadcast_to(ii["L0_ffn_ln_b"], (128, 1024))),
        "ident": np.eye(128, dtype=np.float32),
        "ut2": np.vstack([np.triu(np.ones((64, 64), np.float32))] * 2),
        "consts": P(np.broadcast_to(np.array([PI / 2, 0.5, 0.0, 0.0], np.float32), (128, 4))),
    }
    x = P(ii["L0_attn_x"])
    in_maps = [dict(com, x=x[c]) for c in range(8)]

    if _NC_CACHE is None:
        _NC_CACHE = build_nc()
    res = run_bass_kernel_spmd(_NC_CACHE, in_maps, core_ids=list(range(8)))
    out = np.stack([np.asarray(r["y"]) for r in res.results], axis=0)
    return out.astype(np.float32)


# version 06
# speedup vs baseline: 1.1098x; 1.0852x over previous
import numpy as np
import concourse.bass as bass
import concourse.mybir as mybir
from concourse.bass_utils import run_bass_kernel_spmd

F32 = mybir.dt.float32
BF16 = mybir.dt.bfloat16
I32 = mybir.dt.int32
AF = mybir.ActivationFunctionType
ALU = mybir.AluOpType
AX = mybir.AxisListType

D = 1024
HD = 4096
E = 8
S_CORE = 8192
T = 1024
NSUP = S_CORE // T
NTB = T // 128
KD = D // 128
PI = float(np.pi)
INV2PI = float(1.0 / (2 * np.pi))
TWOPI = float(2 * np.pi)
MAGIC = 0x5F3759DF

ENGINES = ("sync", "gpsimd", "scalar", "vector", "tensor")
ESEM = {"sync": "sd", "gpsimd": "gd", "scalar": "se", "vector": "ve", "tensor": "pe"}


class Buf:
    __slots__ = ("name", "ready", "readers")

    def __init__(self, name):
        self.name = name
        self.ready = None
        self.readers = []


class Op:
    __slots__ = ("eng", "fn", "waits", "sem", "n", "need_inc", "val", "async_done")

    def __init__(self, eng, fn, sem, n, async_done):
        self.eng = eng
        self.fn = fn
        self.waits = []
        self.sem = sem
        self.n = n
        self.need_inc = False
        self.val = None
        self.async_done = async_done


class Prog:
    def __init__(self):
        self.q = {e: [] for e in ENGINES}

    def emit(self, eng, fn, reads=(), writes=(), dma=False, sem=None):
        sem = sem or ESEM[eng]
        op = Op(eng, fn, sem, 16 if dma else 1, dma)
        deps = []
        for b in reads:
            if b.ready is not None:
                deps.append(b.ready)
        for w in writes:
            deps.extend(w.readers)
            if w.ready is not None:
                deps.append(w.ready)
        seen = set()
        for d in deps:
            if id(d) in seen:
                continue
            seen.add(id(d))
            op.waits.append(d)
        self.q[eng].append(op)
        for w in writes:
            w.ready = op
            w.readers = []
        for b in reads:
            b.readers.append(op)
        return op

    def finalize(self):
        for q in self.q.values():
            for op in q:
                if op.n == 16:
                    op.need_inc = True
                for d in op.waits:
                    d.need_inc = True
        cnt = {}
        order = []
        for e in ENGINES:
            order.extend(self.q[e])
        for e in ENGINES:
            for op in self.q[e]:
                if op.need_inc:
                    cnt[op.sem] = cnt.get(op.sem, 0) + op.n
                    op.val = cnt[op.sem]
        self.final_counts = cnt


def build_nc(dbg=False, stages=('X','A1','A2','A3','F0','F1','F2'), nsup=NSUP):
    nc = bass.Bass("TRN2", target_bir_lowering=False)

    dr = {}
    def din(name, shape, dtype=F32):
        dr[name] = nc.dram_tensor(name, list(shape), dtype, kind="ExternalInput").ap()
        return dr[name]

    x_h = din("x", (S_CORE, D))
    wk_h = din("wk", (D, D)); wv_h = din("wv", (D, D)); wq_h = din("wq", (D, D))
    wkp_h = din("wkp", (D, D)); wqp_h = din("wqp", (D, D)); woa_h = din("woa", (D, D))
    w1_h = din("w1", (E, D, HD)); w2_h = din("w2", (E, D, HD)); wof_h = din("wof", (E, HD, D))
    rw_p_h = din("rw_p", (128, KD * E))
    bk_p_h = din("bk_p", (128, KD)); bq_p_h = din("bq_p", (128, KD)); bv_p_h = din("bv_p", (128, KD))
    fg_p_h = din("fg_p", (128, KD)); fb_p_h = din("fb_p", (128, KD))
    eg_p_h = din("eg_p", (128, E * KD)); eb_p_h = din("eb_p", (128, E * KD))
    b1_p_h = din("b1_p", (128, E * 32)); b2_p_h = din("b2_p", (128, E * 32))
    bkp_r_h = din("bkp_r", (1, D)); bqp_r_h = din("bqp_r", (1, D)); bo_r_h = din("bo_r", (1, D))
    bof_r_h = din("bof_r", (E, D))
    ps_bc_h = din("ps_bc", (128, D))
    lng_bc_h = din("lng_bc", (128, D)); lnb_bc_h = din("lnb_bc", (128, D))
    fg_bc_h = din("fg_bc", (128, D)); fb_bc_h = din("fb_bc", (128, D))
    ident_h = din("ident", (128, 128))
    ut2_h = din("ut2", (128, 64))
    consts_h = din("consts", (128, 8))
    y_h = nc.dram_tensor("y", [S_CORE, D], F32, kind="ExternalOutput").ap()
    dbg_h = {}
    if dbg:
        for nm, sh, dt in [
            ("d_xn", (128, 8192), BF16), ("d_fm", (128, 8192), BF16),
            ("d_kfm", (128, 8192), BF16), ("d_qfm", (128, 8192), BF16),
            ("d_vtok", (128, 8192), BF16), ("d_gates", (128, 64), F32),
            ("d_stats", (128, 64), F32), ("d_s", (128, 8192), BF16),
            ("d_yfm", (128, 8192), BF16), ("d_acc", (128, 8192), F32),
            ("d_tanhk", (128, 1024), F32), ("d_kp", (128, 1024), F32),
            ("d_sink", (128, 1024), F32), ("d_cosk", (128, 1024), F32),
            ("d_br", (128, 1024), BF16), ("d_ret", (128, 1024), F32),
            ("d_nrm", (128, 1024), BF16), ("d_xf", (128, 1024), F32),
            ("d_h0", (128, 1024), BF16), ("d_z2", (128, 1024), F32),
            ("d_sinq", (128, 1024), F32), ("d_cosq", (128, 1024), F32),
            ("d_mr", (128, 1024), F32),
        ]:
            dbg_h[nm] = nc.dram_tensor(nm, list(sh), dt, kind="ExternalOutput").ap()

    from contextlib import ExitStack
    es = ExitStack()
    with es:
        sb = lambda name, shape, dt=F32: es.enter_context(nc.sbuf_tensor(name, list(shape), dt))
        ps = lambda name, shape, dt=F32: es.enter_context(nc.psum_tensor(name, list(shape), dt))

        g1 = sb("g1", (128, 6 * 1024))
        g1b = g1[:].bitcast(BF16)
        xn_tok = sb("xn_tok", (128, NTB * 1024), BF16)
        fm = sb("fm", (128, KD * 1024), BF16)
        g2 = sb("g2", (128, 8 * 1024))
        g2b = g2[:].bitcast(BF16)
        vx = sb("vx", (128, NTB * 1024), BF16)
        wkb0 = sb("wkb0", (128, KD * 1024), BF16)
        wkb1 = sb("wkb1", (128, KD * 1024), BF16)
        wkb1f = wkb1[:].bitcast(F32)
        s_sb = sb("s_sb", (128, NTB * 1024), BF16)
        xio0 = sb("xio0", (128, 1024)); xio1 = sb("xio1", (128, 1024))
        htmp = sb("htmp", (128, 1024))
        htmp2 = sb("htmp2", (128, 1024))
        vfm = sb("vfm", (128, 2 * 1024), BF16)
        stats = sb("stats", (128, 96))
        gates = sb("gates", (128, NTB * E))
        gatesT = sb("gatesT", (8, 1024), BF16)
        gbf = sb("gbf", (128, 8), BF16)
        bkp_hi = sb("bkp_hi", (1, D), BF16); bkp_lo = sb("bkp_lo", (1, D), BF16)
        bqp_hi = sb("bqp_hi", (1, D), BF16); bqp_lo = sb("bqp_lo", (1, D), BF16)
        bo_hi = sb("bo_hi", (1, D), BF16); bo_lo = sb("bo_lo", (1, D), BF16)
        bof_bf = sb("bof_bf", (E, D), BF16)
        ps_bc = sb("ps_bc_s", (128, D))
        lng_bc = sb("lng_bc_s", (128, D), BF16); lnb_bc = sb("lnb_bc_s", (128, D), BF16)
        fg_bc = sb("fg_bc_s", (128, D), BF16); fb_bc = sb("fb_bc_s", (128, D), BF16)
        bk_p = sb("bk_p_s", (128, KD)); bq_p = sb("bq_p_s", (128, KD)); bv_p = sb("bv_p_s", (128, KD))
        fg_p = sb("fg_p_s", (128, KD)); fb_p = sb("fb_p_s", (128, KD))
        fb_cb = sb("fb_cb", (128, KD), BF16)
        eg_p = sb("eg_p_s", (128, E * KD)); eb_p = sb("eb_p_s", (128, E * KD))
        b1_p = sb("b1_p_s", (128, E * 32)); b2_p = sb("b2_p_s", (128, E * 32))
        rw_p = sb("rw_p_s", (128, KD * E))
        rw_bf = sb("rw_bf", (128, KD * E), BF16)
        rwg = sb("rwg", (128, KD * E), BF16)
        brw_row = sb("brw_row", (1, E)); brw_rb = sb("brw_rb", (1, E), BF16)
        brw_bc = sb("brw_bc", (128, E))
        ident_b = sb("ident_b", (128, 128), BF16)
        ut2_b = sb("ut2_b", (128, 64), BF16)
        ones_c = sb("ones_c", (1, 128), BF16)
        cs = sb("cs", (128, 8))

        pm1 = ps("pm1", (128, 1024))
        pm2 = ps("pm2", (128, 1024))
        pm3 = ps("pm3", (128, 1024))
        ps1 = ps("ps1", (128, 512))
        pt = ps("pt", (128, 1024), BF16)
        pt32 = pt[:].bitcast(F32)

        sems = {}
        for s in ("sd", "gd", "se", "ve", "pe", "so"):
            sems[s] = es.enter_context(nc.semaphore(s))

        nc.const_aps.register(F32, PI / 2, cs[:, 0:1])
        nc.const_aps.register(F32, 0.5, cs[:, 1:2])

        P = Prog()
        B = lambda n: Buf(n)
        b_xn_tok = B("xn_tok"); b_fm = B("fm"); b_kfm = B("kfm"); b_qfm = B("qfm")
        b_vx = B("vx"); b_wkb0 = B("wkb0"); b_wkb1 = B("wkb1"); b_s = B("s")
        b_xio = [B("xio0"), B("xio1")]; b_htmp = B("htmp")
        b_vfm = [B("vfm0"), B("vfm1")]
        b_stP = B("stP"); b_stT = [B("stE"), B("stO")]
        b_gates = B("gates"); b_gatesT = B("gatesT"); b_gbf = B("gbf")
        b_acc = B("acc")
        b_slot = [B(f"S{i}") for i in range(6)]
        b_pm1 = B("pm1"); b_pm2 = B("pm2"); b_pm3 = B("pm3"); b_ps1 = B("ps1"); b_pt = B("pt")
        b_consts = B("consts")
        b_w1b = [B("w1b0"), B("w1b1")]; b_w2b = [B("w2b0"), B("w2b1")]
        b_wob = [B("wob0"), B("wob1")]
        b_z2 = B("z2"); b_sl = B("sl")
        b_z2o = B("z2o"); b_slo = B("slo"); b_htmp2 = B("htmp2")
        b_h = [B(f"h{i}") for i in range(8)]
        b_y = B("y_dummy")

        SLOT = [g1[:, i * 1024:(i + 1) * 1024] for i in range(6)]
        SLOTB = [g1b[:, i * 2048:(i + 1) * 2048] for i in range(6)]
        h_v = g1b[:].rearrange("p (hc t) -> p hc t", hc=12)
        xn_v = xn_tok[:].rearrange("p (tb t) -> p tb t", tb=NTB)
        fm_v = fm[:].rearrange("p (k t) -> p k t", k=KD)
        kfm_v = g2b[:, 0:KD * 1024].rearrange("p (k t) -> p k t", k=KD)
        qfm_v = g2b[:, KD * 1024:2 * KD * 1024].rearrange("p (k t) -> p k t", k=KD)
        acc_v = g2[:].rearrange("p (tb t) -> p tb t", tb=NTB)
        vtok_v = vx[:].rearrange("p (tb t) -> p tb t", tb=NTB)
        xe_v = vx[:].rearrange("p (k t) -> p k t", k=KD)
        ytmp = vx[:, 0:1024]
        s_v = s_sb[:].rearrange("p (tb t) -> p tb t", tb=NTB)
        wkb0_v = wkb0[:].rearrange("p (k t) -> p k t", k=KD)
        wkb1_v = wkb1[:].rearrange("p (k t) -> p k t", k=KD)
        w1b_v = [wkb0[:, 0:2048].rearrange("p (k h) -> p k h", k=KD),
                 wkb0[:, 2048:4096].rearrange("p (k h) -> p k h", k=KD)]
        w2b_v = [wkb0[:, 4096:6144].rearrange("p (k h) -> p k h", k=KD),
                 wkb0[:, 6144:8192].rearrange("p (k h) -> p k h", k=KD)]
        woa_slab = [wkb1[:, 0:1024], wkb1[:, 1024:2048]]
        wob_vv = [wkb1[:, 0:2048].rearrange("p (c t) -> p c t", c=2),
                  wkb1[:, 2048:4096].rearrange("p (c t) -> p c t", c=2)]
        z2f = wkb1f[:, 2048:3072]
        slf = wkb1f[:, 3072:4096]
        xio = [xio0, xio1]

        C_VARX, C_RSTD, C_RSTD2 = 8, 16, 24
        TB0 = {0: 32, 1: 64}
        SX, SQ, MEAN, T0, T1, T2, T3, NV, NH, M1, M2, P1, P2, MU3, RSTD3, NMU, RMUL = range(17)
        SC = lambda c: stats[:, c:c + 1]
        TC = lambda par, off: stats[:, TB0[par] + off:TB0[par] + off + 1]

        def EM(eng, fn, reads=(), writes=(), dma=False, sem=None):
            return P.emit(eng, fn, reads, writes, dma, sem)

        def DBG(name, src_ap, rbuf):
            if dbg and name in dbg_h:
                EM("sync", d(dbg_h[name], src_ap), reads=[rbuf], dma=True, sem="so")

        def d(dst, src):
            return lambda e, dst=dst, src=src: e.dma_start(out=dst, in_=src)
        for dst, src in [(ps_bc[:], ps_bc_h), (bk_p[:], bk_p_h), (bq_p[:], bq_p_h),
                         (bv_p[:], bv_p_h), (fg_p[:], fg_p_h), (fb_p[:], fb_p_h),
                         (eg_p[:], eg_p_h), (eb_p[:], eb_p_h), (b1_p[:], b1_p_h),
                         (b2_p[:], b2_p_h), (rw_p[:], rw_p_h), (cs[:], consts_h)]:
            EM("sync", d(dst, src), writes=[b_consts], dma=True)
        for dst, src in [(lng_bc[:], lng_bc_h), (lnb_bc[:], lnb_bc_h),
                         (fg_bc[:], fg_bc_h), (fb_bc[:], fb_bc_h),
                         (ident_b[:], ident_h), (ut2_b[:], ut2_h)]:
            EM("gpsimd", d(dst, src), writes=[b_consts], dma=True)
        EM("vector", lambda e: e.memset(ones_c[:], 1.0), writes=[b_consts])

        rowsrc = xio0[0:1, 0:1024]
        rowtmp = xio1[0:1, 0:1024]
        for src_h, hi, lo in [(bkp_r_h, bkp_hi, bkp_lo), (bqp_r_h, bqp_hi, bqp_lo),
                              (bo_r_h, bo_hi, bo_lo)]:
            EM("sync", d(rowsrc, src_h), writes=[b_xio[0]], dma=True)
            EM("vector", lambda e, hi=hi: e.tensor_copy(hi[:], rowsrc),
               reads=[b_xio[0]], writes=[b_consts])
            EM("vector", lambda e, hi=hi: e.tensor_copy(rowtmp, hi[:]),
               reads=[b_consts], writes=[b_xio[1]])
            EM("vector", lambda e: e.tensor_sub(rowtmp, rowsrc, rowtmp),
               reads=[b_xio[0], b_xio[1]], writes=[b_xio[1]])
            EM("vector", lambda e, lo=lo: e.tensor_copy(lo[:], rowtmp),
               reads=[b_xio[1]], writes=[b_consts])
        EM("sync", d(xio0[0:8, 0:1024], bof_r_h), writes=[b_xio[0]], dma=True)
        EM("vector", lambda e: e.tensor_copy(bof_bf[:], xio0[0:8, 0:1024]),
           reads=[b_xio[0]], writes=[b_consts])
        EM("vector", lambda e: e.tensor_copy(rw_bf[:], rw_p[:]),
           reads=[b_consts], writes=[b_consts])
        for k in range(KD):
            EM("vector", lambda e, k=k: e.tensor_scalar(
                out=rwg[:, k * E:(k + 1) * E], in0=rw_p[:, k * E:(k + 1) * E],
                scalar1=fg_p[:, k:k + 1], scalar2=None, op0=ALU.mult),
               reads=[b_consts], writes=[b_consts])
        EM("vector", lambda e: e.tensor_copy(fb_cb[:], fb_p[:]),
           reads=[b_consts], writes=[b_consts])
        def brw_mm(e):
            for k in range(KD):
                inst = e.matmul(ps1[0:1, 0:E], fb_cb[:, k:k + 1], rw_bf[:, k * E:(k + 1) * E],
                                start=(k == 0), stop=(k == KD - 1))
            return inst
        EM("tensor", brw_mm, reads=[b_consts], writes=[b_ps1])
        EM("vector", lambda e: e.tensor_copy(brw_row[:], ps1[0:1, 0:E]),
           reads=[b_ps1], writes=[b_consts])
        EM("vector", lambda e: e.tensor_copy(brw_rb[:], brw_row[:]),
           reads=[b_consts], writes=[b_consts])
        EM("tensor", lambda e: e.matmul(ps1[0:128, 0:E], ones_c[:], brw_rb[:],
                                        start=True, stop=True),
           reads=[b_consts], writes=[b_ps1])
        EM("vector", lambda e: e.tensor_copy(brw_bc[:], ps1[0:128, 0:E]),
           reads=[b_ps1], writes=[b_consts])

        def newton_rsqrt(dst_col, src_col, eps, par, extra_reads=(), b_stats=None):
            v = TC(par, NV); h = TC(par, NH)
            b_stats = b_stats or b_stT[par]
            rd = [b_stats] + list(extra_reads)
            EM("vector", lambda e: e.tensor_scalar(out=v, in0=src_col, scalar1=eps,
                                                   scalar2=None, op0=ALU.add),
               reads=rd, writes=[b_stats])
            iv = v.bitcast(I32); iy = dst_col.bitcast(I32)
            EM("vector", lambda e: e.tensor_scalar(out=iy, in0=iv, scalar1=1,
                                                   scalar2=None, op0=ALU.logical_shift_right),
               reads=[b_stats], writes=[b_stats])
            EM("vector", lambda e: e.tensor_scalar(out=iy, in0=iy, scalar1=-1, scalar2=MAGIC,
                                                   op0=ALU.mult, op1=ALU.add),
               reads=[b_stats], writes=[b_stats])
            for _ in range(3):
                EM("vector", lambda e: e.tensor_scalar(out=h, in0=dst_col, scalar1=dst_col,
                                                       scalar2=None, op0=ALU.mult),
                   reads=[b_stats], writes=[b_stats])
                EM("vector", lambda e: e.tensor_scalar(out=h, in0=h, scalar1=v,
                                                       scalar2=None, op0=ALU.mult),
                   reads=[b_stats], writes=[b_stats])
                EM("vector", lambda e: e.tensor_scalar(out=h, in0=h, scalar1=-0.5, scalar2=1.5,
                                                       op0=ALU.mult, op1=ALU.add),
                   reads=[b_stats], writes=[b_stats])
                EM("vector", lambda e: e.tensor_scalar(out=dst_col, in0=dst_col, scalar1=h,
                                                       scalar2=None, op0=ALU.mult),
                   reads=[b_stats], writes=[b_stats])

        def transpose8(src_ap, dst_ap, src_bufs, dst_buf, tiles=8):
            def tp(e):
                for k in range(tiles):
                    inst = e.transpose(pt[:, k * 128:(k + 1) * 128],
                                       src_ap[:, k * 128:(k + 1) * 128], ident_b[:])
                return inst
            EM("tensor", tp, reads=list(src_bufs) + [b_consts], writes=[b_pt])
            EM("vector", lambda e: e.tensor_copy(dst_ap, pt[:].rearrange(
                "p (k t) -> p k t", k=8)[:, 0:tiles, :]),
               reads=[b_pt], writes=[dst_buf])

        def phase_chain(src_slot, sin_dst, cos_dst, t_u, t_g):
            sp = SLOT[src_slot]; u = SLOT[t_u]; gg = SLOT[t_g]
            bs, bu, bg = b_slot[src_slot], b_slot[t_u], b_slot[t_g]
            for shift, dst in ((0.0, sin_dst), (0.25, cos_dst)):
                EM("vector", lambda e, sh=shift: e.tensor_scalar(
                    out=u, in0=sp, scalar1=INV2PI, scalar2=0.5 + sh,
                    op0=ALU.mult, op1=ALU.add), reads=[bs], writes=[bu])
                EM("vector", lambda e: e.tensor_scalar(
                    out=gg, in0=u, scalar1=1.0, scalar2=None, op0=ALU.is_ge),
                   reads=[bu], writes=[bg])
                EM("vector", lambda e: e.tensor_scalar(
                    out=u, in0=u, scalar1=0.0, scalar2=None, op0=ALU.is_lt),
                   reads=[bu], writes=[bu])
                EM("vector", lambda e: e.tensor_sub(gg, gg, u),
                   reads=[bu, bg], writes=[bg])
                EM("vector", lambda e: e.scalar_tensor_tensor(
                    out=u, in0=gg, scalar=-TWOPI, in1=sp, op0=ALU.mult, op1=ALU.add),
                   reads=[bg, bs], writes=[bu])
                dd = SLOT[dst]; bd = b_slot[dst]
                if shift == 0.0:
                    EM("scalar", lambda e, dd=dd: e.activation(out=dd, in_=u, func=AF.Sin),
                       reads=[bu], writes=[bd])
                else:
                    EM("scalar", lambda e, dd=dd: e.activation(out=dd, in_=u, func=AF.Sin,
                                                               bias=PI / 2),
                       reads=[bu, b_consts], writes=[bd])

        for sup in range(nsup):
            t0 = sup * T
            EM("gpsimd", d(wkb0_v[:], wk_h.rearrange("(k p) m -> p k m", p=128)),
               writes=[b_wkb0], dma=True)
            EM("gpsimd", d(wkb1_v[:], wq_h.rearrange("(k p) m -> p k m", p=128)),
               writes=[b_wkb1], dma=True)

            for tb in range(NTB):
                par = tb % 2
                xb = xio[par]; bxb = b_xio[par]
                bst = b_stT[par]
                junk = SLOT[5 - par]
                EM("sync", d(xb[:], x_h[t0 + tb * 128: t0 + (tb + 1) * 128, :]),
                   writes=[bxb], dma=True)
                EM("vector", lambda e, xb=xb, par=par: e.reduce_sum(TC(par, SX), xb[:], axis=AX.X),
                   reads=[bxb], writes=[bst])
                EM("scalar", lambda e, xb=xb, par=par, junk=junk: e.activation(
                    out=junk, in_=xb[:], func=AF.Square, accum_out=TC(par, SQ)),
                   reads=[bxb], writes=[b_slot[5 - par], bst])
                EM("vector", lambda e, par=par: e.tensor_scalar(out=TC(par, MEAN), in0=TC(par, SX),
                                                       scalar1=1.0 / D, scalar2=None, op0=ALU.mult),
                   reads=[bst], writes=[bst])
                EM("vector", lambda e, par=par: e.tensor_scalar(out=TC(par, T0), in0=TC(par, MEAN),
                                                       scalar1=TC(par, MEAN),
                                                       scalar2=None, op0=ALU.mult),
                   reads=[bst], writes=[bst])
                EM("vector", lambda e, tb=tb, par=par: e.scalar_tensor_tensor(
                    out=SC(C_VARX + tb), in0=TC(par, SQ), scalar=1.0 / D, in1=TC(par, T0),
                    op0=ALU.mult, op1=ALU.subtract),
                   reads=[bst], writes=[bst, b_stP])
                newton_rsqrt(SC(C_RSTD + tb), SC(C_VARX + tb), 1e-5, par,
                             extra_reads=[b_stP])
                EM("vector", lambda e, par=par: e.tensor_scalar(out=TC(par, T1), in0=TC(par, MEAN),
                                                       scalar1=-1.0, scalar2=None, op0=ALU.mult),
                   reads=[bst], writes=[bst])
                EM("vector", lambda e, tb=tb, xb=xb, par=par: e.tensor_scalar(
                    out=xn_v[:, tb, :], in0=xb[:], scalar1=TC(par, T1),
                    scalar2=SC(C_RSTD + tb), op0=ALU.add, op1=ALU.mult),
                   reads=[bxb, bst, b_stP], writes=[b_xn_tok])
                EM("vector", lambda e, tb=tb, par=par: e.tensor_scalar(
                    out=TC(par, T2), in0=SC(C_RSTD + tb), scalar1=SC(C_RSTD + tb),
                    scalar2=None, op0=ALU.mult),
                   reads=[bst, b_stP], writes=[bst])
                EM("vector", lambda e, tb=tb, par=par: e.tensor_scalar(
                    out=TC(par, T3), in0=SC(C_VARX + tb), scalar1=TC(par, T2),
                    scalar2=None, op0=ALU.mult),
                   reads=[bst, b_stP], writes=[bst])
                newton_rsqrt(TC(par, NMU), TC(par, T3), 1e-5, par)
                EM("vector", lambda e, tb=tb, par=par: e.tensor_scalar(
                    out=SC(C_RSTD2 + tb), in0=TC(par, NMU), scalar1=1.0,
                    scalar2=None, op0=ALU.mult),
                   reads=[bst], writes=[b_stP])
                transpose8(xn_v[:, tb, :],
                           fm_v[:, :, tb * 128:(tb + 1) * 128],
                           [b_xn_tok], b_fm)

            if dbg and sup == 0:
                DBG("d_xn", xn_tok[:], b_xn_tok)
                DBG("d_fm", fm[:], b_fm)
                DBG("d_stats", stats[:, 0:64], b_stP)
            if 'A1' not in stages:
                continue
            def qkv(wbuf_v, bwbuf, bias_p, bbias, dstv, bdst):
                for m in range(KD):
                    pmm = pm1 if (m % 2 == 0) else pm2
                    bp = b_pm1 if (m % 2 == 0) else b_pm2
                    def grp(e, m=m, pmm=pmm, wbuf_v=wbuf_v):
                        for sub in range(2):
                            for k in range(KD):
                                inst = e.matmul(pmm[:, sub * 512:(sub + 1) * 512],
                                                wbuf_v[:, k, m * 128:(m + 1) * 128],
                                                fm_v[:, k, sub * 512:(sub + 1) * 512],
                                                start=(k == 0), stop=(k == KD - 1))
                        return inst
                    EM("tensor", grp, reads=[bwbuf, b_fm], writes=[bp])
                    EM("scalar", lambda e, m=m, pmm=pmm, dstv=dstv, bias_p=bias_p:
                       e.activation(out=dstv[:, m, :], in_=pmm[:], func=AF.Identity,
                                    bias=bias_p[:, m:m + 1]),
                       reads=[bp, bbias], writes=[bdst])
            qkv(wkb0_v, b_wkb0, bk_p, b_consts, kfm_v, b_kfm)
            qkv(wkb1_v, b_wkb1, bq_p, b_consts, qfm_v, b_qfm)
            EM("gpsimd", d(wkb0_v[:], wv_h.rearrange("(k p) m -> p k m", p=128)),
               writes=[b_wkb0], dma=True)
            for m in range(KD):
                pmm = pm1 if (m % 2 == 0) else pm2
                bp = b_pm1 if (m % 2 == 0) else b_pm2
                vf = vfm[:, (m % 2) * 1024:((m % 2) + 1) * 1024]
                bvf = b_vfm[m % 2]
                def grpv(e, m=m, pmm=pmm):
                    for sub in range(2):
                        for k in range(KD):
                            inst = e.matmul(pmm[:, sub * 512:(sub + 1) * 512],
                                            wkb0_v[:, k, m * 128:(m + 1) * 128],
                                            fm_v[:, k, sub * 512:(sub + 1) * 512],
                                            start=(k == 0), stop=(k == KD - 1))
                    return inst
                EM("tensor", grpv, reads=[b_wkb0, b_fm], writes=[bp])
                EM("scalar", lambda e, pmm=pmm, vf=vf, m=m: e.activation(
                    out=vf, in_=pmm[:], func=AF.Identity, bias=bv_p[:, m:m + 1]),
                   reads=[bp, b_consts], writes=[bvf])
                def tpv(e, vf=vf):
                    for tb in range(NTB):
                        inst = e.transpose(pt[:, tb * 128:(tb + 1) * 128],
                                           vf[:, tb * 128:(tb + 1) * 128], ident_b[:])
                    return inst
                EM("tensor", tpv, reads=[bvf, b_consts], writes=[b_pt])
                EM("vector", lambda e, m=m: e.tensor_copy(
                    vtok_v[:, :, m * 128:(m + 1) * 128],
                    pt[:].rearrange("p (tb t) -> p tb t", tb=8)),
                   reads=[b_pt], writes=[b_vx])
            EM("gpsimd", d(wkb0_v[:], wkp_h.rearrange("(k p) m -> p k m", p=128)),
               writes=[b_wkb0], dma=True)
            EM("gpsimd", d(wkb1_v[:], wqp_h.rearrange("(k p) m -> p k m", p=128)),
               writes=[b_wkb1], dma=True)
            for tb in range(NTB):
                par = tb % 2
                bst = b_stT[par]
                def lgrp(e, tb=tb):
                    for k in range(KD):
                        inst = e.matmul(ps1[:, 0:E], fm_v[:, k, tb * 128:(tb + 1) * 128],
                                        rwg[:, k * E:(k + 1) * E],
                                        start=(k == 0), stop=(k == KD - 1))
                    return inst
                EM("tensor", lgrp, reads=[b_fm, b_consts], writes=[b_ps1])
                gsl = gates[:, tb * E:(tb + 1) * E]
                EM("vector", lambda e, tb=tb, gsl=gsl: e.scalar_tensor_tensor(
                    out=gsl, in0=ps1[:, 0:E], scalar=SC(C_RSTD2 + tb), in1=brw_bc[:],
                    op0=ALU.mult, op1=ALU.add),
                   reads=[b_ps1, b_stP, b_consts], writes=[b_gates])
                s0, s1, s2 = (0, 1, 2) if par == 0 else (3, 4, 5)
                e1 = SLOT[s0][:, 0:E]; l2 = SLOT[s1][:, 0:E]; e2 = SLOT[s2][:, 0:E]
                EM("vector", lambda e, gsl=gsl, par=par: e.reduce_max(TC(par, M1), gsl, axis=AX.X),
                   reads=[b_gates], writes=[bst])
                EM("vector", lambda e, gsl=gsl, e1=e1, par=par: e.tensor_scalar(
                    out=e1, in0=gsl, scalar1=TC(par, M1), scalar2=None, op0=ALU.is_equal),
                   reads=[b_gates, bst], writes=[b_slot[s0]])
                EM("vector", lambda e, gsl=gsl, e1=e1, l2=l2: e.scalar_tensor_tensor(
                    out=l2, in0=e1, scalar=-1e30, in1=gsl, op0=ALU.mult, op1=ALU.add),
                   reads=[b_slot[s0], b_gates], writes=[b_slot[s1]])
                EM("vector", lambda e, l2=l2, par=par: e.reduce_max(TC(par, M2), l2, axis=AX.X),
                   reads=[b_slot[s1]], writes=[bst])
                EM("vector", lambda e, par=par: e.tensor_scalar(out=TC(par, T0), in0=TC(par, M1),
                                                       scalar1=TC(par, M2), scalar2=None,
                                                       op0=ALU.subtract),
                   reads=[bst], writes=[bst])
                EM("scalar", lambda e, par=par: e.activation(out=TC(par, T1), in_=TC(par, T0),
                                                    func=AF.Tanh, scale=0.5),
                   reads=[bst, b_consts], writes=[bst])
                EM("vector", lambda e, par=par: e.tensor_scalar(out=TC(par, P1), in0=TC(par, T1),
                                                       scalar1=0.5, scalar2=0.5,
                                                       op0=ALU.mult, op1=ALU.add),
                   reads=[bst], writes=[bst])
                EM("vector", lambda e, par=par: e.tensor_scalar(out=TC(par, P2), in0=TC(par, P1),
                                                       scalar1=-1.0, scalar2=1.0,
                                                       op0=ALU.mult, op1=ALU.add),
                   reads=[bst], writes=[bst])
                EM("vector", lambda e, l2=l2, e2=e2, par=par: e.tensor_scalar(
                    out=e2, in0=l2, scalar1=TC(par, M2), scalar2=None, op0=ALU.is_equal),
                   reads=[b_slot[s1], bst], writes=[b_slot[s2]])
                EM("vector", lambda e, e1=e1, par=par: e.tensor_scalar(
                    out=e1, in0=e1, scalar1=TC(par, P1), scalar2=None, op0=ALU.mult),
                   reads=[b_slot[s0], bst], writes=[b_slot[s0]])
                EM("vector", lambda e, gsl=gsl, e1=e1, e2=e2, par=par: e.scalar_tensor_tensor(
                    out=gsl, in0=e2, scalar=TC(par, P2), in1=e1, op0=ALU.mult, op1=ALU.add),
                   reads=[b_slot[s0], b_slot[s2], bst], writes=[b_gates])

            if dbg and sup == 0:
                DBG("d_kfm", g2b[:, 0:KD * 1024], b_kfm)
                DBG("d_qfm", g2b[:, KD * 1024:2 * KD * 1024], b_qfm)
                DBG("d_vtok", vx[:], b_vx)
                DBG("d_gates", gates[:], b_gates)
            if 'A2' not in stages:
                continue
            PIH = float(np.pi / 2); PI3H = float(3 * np.pi / 2)

            def a2_phase(src_slot, sin_dst, cos_dst, ta, tg2):
                sp = SLOT[src_slot]; bs = b_slot[src_slot]
                for shift, dsti in ((0.0, sin_dst), (1.0, cos_dst)):
                    chi = 2 if shift == 0.0 else 4
                    clo = 3 if shift == 0.0 else 5
                    u = SLOT[ta]; g2 = SLOT[tg2]
                    bu, bg = b_slot[ta], b_slot[tg2]
                    EM("vector", lambda e, u=u, chi=chi: e.tensor_scalar(
                        out=u, in0=sp, scalar1=cs[:, chi:chi + 1], scalar2=None,
                        op0=ALU.is_ge),
                       reads=[bs, b_consts], writes=[bu])
                    EM("vector", lambda e, g2=g2, clo=clo: e.tensor_scalar(
                        out=g2, in0=sp, scalar1=cs[:, clo:clo + 1], scalar2=None,
                        op0=ALU.is_lt),
                       reads=[bs, b_consts], writes=[bg])
                    EM("gpsimd", lambda e, u=u, g2=g2: e.tensor_sub(u, u, g2),
                       reads=[bu, bg], writes=[bu])
                    EM("vector", lambda e, u=u, g2=g2: e.scalar_tensor_tensor(
                        out=g2, in0=u, scalar=-TWOPI, in1=sp, op0=ALU.mult, op1=ALU.add),
                       reads=[bu, bs], writes=[bg])
                    dd = SLOT[dsti]
                    if shift == 0.0:
                        EM("scalar", lambda e, dd=dd, g2=g2: e.activation(
                            out=dd, in_=g2, func=AF.Sin),
                           reads=[bg], writes=[b_slot[dsti]])
                    else:
                        EM("scalar", lambda e, dd=dd, g2=g2: e.activation(
                            out=dd, in_=g2, func=AF.Sin, bias=PI / 2),
                           reads=[bg, b_consts], writes=[b_slot[dsti]])

            def a2_transpose(tb):
                nb = vfm[:, (tb % 2) * 1024:((tb % 2) + 1) * 1024]
                ptb = pt if tb % 2 == 0 else pm3[:].bitcast(BF16)[:, 0:1024]
                bptb = b_pt if tb % 2 == 0 else b_pm3
                def tp(e, nb=nb, ptb=ptb):
                    for k in range(8):
                        inst = e.transpose(ptb[:, k * 128:(k + 1) * 128],
                                           nb[:, k * 128:(k + 1) * 128], ident_b[:])
                    return inst
                EM("tensor", tp, reads=[b_vfm[tb % 2], b_consts], writes=[bptb])
                EM("vector", lambda e, tb=tb, ptb=ptb: e.tensor_copy(
                    fm_v[:, :, tb * 128:(tb + 1) * 128],
                    ptb.rearrange("p (k t) -> p k t", k=8)),
                   reads=[bptb], writes=[b_fm])

            for tb in range(NTB):
                par = tb % 2
                bst = b_stT[par]
                def pgrp(e, tb=tb, src=kfm_v, wv_=wkb0_v, hi=bkp_hi, lo=bkp_lo, pmm=pm1):
                    for dh in range(2):
                        dsl = slice(dh * 512, (dh + 1) * 512)
                        inst = e.matmul(pmm[:, dsl], ones_c[:], hi[0:1, dsl],
                                        start=True, stop=False)
                        inst = e.matmul(pmm[:, dsl], ones_c[:], lo[0:1, dsl],
                                        start=False, stop=False)
                        for k in range(KD):
                            inst = e.matmul(pmm[:, dsl], src[:, k, tb * 128:(tb + 1) * 128],
                                            wv_[:, k, dsl],
                                            start=False, stop=(k == KD - 1))
                    return inst
                EM("tensor", lambda e, tb=tb: pgrp(e, tb, kfm_v, wkb0_v, bkp_hi, bkp_lo, pm1),
                   reads=[b_kfm, b_wkb0, b_consts], writes=[b_pm1])
                EM("scalar", lambda e: e.activation(out=SLOT[0], in_=pm1[:], func=AF.Tanh),
                   reads=[b_pm1], writes=[b_slot[0]])
                if dbg and sup == 0 and tb == 0:
                    DBG("d_tanhk", SLOT[0], b_slot[0])
                EM("tensor", lambda e, tb=tb: pgrp(e, tb, qfm_v, wkb1_v, bqp_hi, bqp_lo, pm2),
                   reads=[b_qfm, b_wkb1, b_consts], writes=[b_pm2])
                EM("scalar", lambda e: e.activation(out=SLOT[1], in_=pm2[:], func=AF.Tanh),
                   reads=[b_pm2], writes=[b_slot[1]])
                if tb > 0:
                    a2_transpose(tb - 1)
                EM("vector", lambda e: e.tensor_mul(SLOT[0], SLOT[0], ps_bc[:]),
                   reads=[b_slot[0], b_consts], writes=[b_slot[0]])
                if dbg and sup == 0 and tb == 0:
                    DBG("d_kp", SLOT[0], b_slot[0])
                EM("vector", lambda e: e.tensor_mul(SLOT[1], SLOT[1], ps_bc[:]),
                   reads=[b_slot[1], b_consts], writes=[b_slot[1]])
                a2_phase(1, 4, 5, 2, 3)
                a2_phase(0, 1, 2, 2, 3)
                if dbg and sup == 0 and tb == 0:
                    DBG("d_sink", SLOT[1], b_slot[1])
                    DBG("d_cosk", SLOT[2], b_slot[2])
                brb = SLOTB[3][:, 0:1024]; bib = SLOTB[0][:, 0:1024]
                EM("vector", lambda e, tb=tb, brb=brb: e.tensor_mul(
                    brb, vtok_v[:, tb, :], SLOT[2]),
                   reads=[b_vx, b_slot[2]], writes=[b_slot[3]])
                EM("vector", lambda e, tb=tb, bib=bib: e.tensor_mul(
                    bib, vtok_v[:, tb, :], SLOT[1]),
                   reads=[b_vx, b_slot[1]], writes=[b_slot[0]])
                if dbg and sup == 0 and tb == 0:
                    DBG("d_br", brb, b_slot[3])
                def cgrp(e, src2, pmm):
                    for dh in range(2):
                        dsl = slice(dh * 512, (dh + 1) * 512)
                        e.matmul(pmm[0:64, dsl], ut2_b[0:64, :], src2[0:64, dsl],
                                 start=True, stop=True, tile_position=(0, 0))
                        inst = e.matmul(pmm[64:128, dsl], ut2_b[64:128, :], src2[64:128, dsl],
                                        start=True, stop=True, tile_position=(64, 64))
                    return inst
                EM("tensor", lambda e, brb=brb: cgrp(e, brb, pm1),
                   reads=[b_slot[3], b_consts], writes=[b_pm1])
                EM("tensor", lambda e, bib=bib: cgrp(e, bib, pm2),
                   reads=[b_slot[0], b_consts], writes=[b_pm2])
                if dbg and sup == 0 and tb == 0:
                    DBG("d_sinq", SLOT[4], b_slot[4])
                    DBG("d_cosq", SLOT[5], b_slot[5])
                    EM("vector", lambda e: e.tensor_copy(htmp[:], pm1[:]),
                       reads=[b_pm1], writes=[b_htmp])
                    DBG("d_mr", htmp[:], b_htmp)
                EM("vector", lambda e: e.tensor_mul(SLOT[1], pm1[:], SLOT[5]),
                   reads=[b_pm1, b_slot[5]], writes=[b_slot[1]])
                EM("vector", lambda e: e.tensor_mul(SLOT[2], pm2[:], SLOT[4]),
                   reads=[b_pm2, b_slot[4]], writes=[b_slot[2]])
                EM("gpsimd", lambda e: e.tensor_add(SLOT[3], SLOT[1], SLOT[2]),
                   reads=[b_slot[1], b_slot[2]], writes=[b_slot[3]])
                if dbg and sup == 0 and tb == 0:
                    DBG("d_ret", SLOT[3], b_slot[3])
                EM("vector", lambda e, par=par: e.reduce_sum(TC(par, SX), SLOT[3], axis=AX.X),
                   reads=[b_slot[3]], writes=[bst])
                EM("scalar", lambda e, par=par: e.activation(out=SLOT[1], in_=SLOT[3],
                                                    func=AF.Square, accum_out=TC(par, SQ)),
                   reads=[b_slot[3]], writes=[b_slot[1], bst])
                EM("vector", lambda e, par=par: e.tensor_scalar(out=TC(par, MEAN), in0=TC(par, SX),
                                                       scalar1=1.0 / D, scalar2=None,
                                                       op0=ALU.mult),
                   reads=[bst], writes=[bst])
                EM("vector", lambda e, par=par: e.tensor_scalar(out=TC(par, T0), in0=TC(par, MEAN),
                                                       scalar1=TC(par, MEAN), scalar2=None,
                                                       op0=ALU.mult),
                   reads=[bst], writes=[bst])
                EM("vector", lambda e, par=par: e.scalar_tensor_tensor(
                    out=TC(par, T3), in0=TC(par, SQ), scalar=1.0 / D, in1=TC(par, T0),
                    op0=ALU.mult, op1=ALU.subtract),
                   reads=[bst], writes=[bst])
                EM("vector", lambda e, par=par: e.tensor_scalar(out=TC(par, T3), in0=TC(par, T3),
                                                       scalar1=1.0 / D, scalar2=None,
                                                       op0=ALU.mult),
                   reads=[bst], writes=[bst])
                newton_rsqrt(TC(par, RMUL), TC(par, T3), 1e-5, par)
                EM("vector", lambda e, par=par: e.tensor_scalar(out=TC(par, RMUL), in0=TC(par, RMUL),
                                                       scalar1=1.0 / 32.0, scalar2=None,
                                                       op0=ALU.mult),
                   reads=[bst], writes=[bst])
                EM("vector", lambda e, par=par: e.tensor_scalar(out=TC(par, NMU), in0=TC(par, MEAN),
                                                       scalar1=-1.0, scalar2=None,
                                                       op0=ALU.mult),
                   reads=[bst], writes=[bst])
                EM("vector", lambda e, par=par: e.tensor_scalar(
                    out=SLOT[0], in0=SLOT[3], scalar1=TC(par, NMU), scalar2=TC(par, RMUL),
                    op0=ALU.add, op1=ALU.mult),
                   reads=[b_slot[3], bst], writes=[b_slot[0]])
                EM("gpsimd", lambda e: e.tensor_mul(SLOT[1], SLOT[0], lng_bc[:]),
                   reads=[b_slot[0], b_consts], writes=[b_slot[1]])
                nrmb = vfm[:, par * 1024:(par + 1) * 1024]
                EM("gpsimd", lambda e, nrmb=nrmb: e.tensor_add(nrmb, SLOT[1], lnb_bc[:]),
                   reads=[b_slot[1], b_consts], writes=[b_vfm[par]])
                if dbg and sup == 0 and tb == 0:
                    DBG("d_nrm", nrmb, b_vfm[par])
            a2_transpose(NTB - 1)

            if 'A3' not in stages:
                continue
            for grp in range(2):
                tbs = list(range(grp * 4, grp * 4 + 4))
                pms = [(pm1, b_pm1), (pm2, b_pm2), (pm3, b_pm3), None]
                for k in range(KD):
                    par = k % 2
                    EM("gpsimd", d(woa_slab[par][:], woa_h[k * 128:(k + 1) * 128, :]),
                       writes=[b_wob[par]], dma=True)
                    for i, tb in enumerate(tbs):
                        def sgrp(e, k=k, i=i, tb=tb, par=par):
                            for dh in range(2):
                                dsl = slice(dh * 512, (dh + 1) * 512)
                                if i < 3:
                                    dst = pms[i][0][:, dsl]
                                else:
                                    dst = ps1[:, 0:512] if dh == 0 else pt32[:, 0:512]
                                if k == 0:
                                    e.matmul(dst, ones_c[:], bo_hi[0:1, dsl],
                                             start=True, stop=False)
                                    e.matmul(dst, ones_c[:], bo_lo[0:1, dsl],
                                             start=False, stop=False)
                                inst = e.matmul(dst, fm_v[:, k, tb * 128:(tb + 1) * 128],
                                                woa_slab[par][:, dsl],
                                                start=False, stop=(k == KD - 1))
                            return inst
                        wr = [pms[i][1]] if i < 3 else [b_ps1, b_pt]
                        EM("tensor", sgrp, reads=[b_fm, b_wob[par], b_consts], writes=wr)
                for i, tb in enumerate(tbs):
                    if i < 3:
                        EM("vector", lambda e, i=i, tb=tb: e.tensor_tensor(
                            out=s_v[:, tb, :], in0=pms[i][0][:], in1=xn_v[:, tb, :],
                            op=ALU.add),
                           reads=[pms[i][1], b_xn_tok], writes=[b_s])
                    else:
                        EM("vector", lambda e, tb=tb: e.tensor_tensor(
                            out=s_v[:, tb, 0:512], in0=ps1[:, 0:512],
                            in1=xn_v[:, tb, 0:512], op=ALU.add),
                           reads=[b_ps1, b_xn_tok], writes=[b_s])
                        EM("vector", lambda e, tb=tb: e.tensor_tensor(
                            out=s_v[:, tb, 512:1024], in0=pt32[:, 0:512],
                            in1=xn_v[:, tb, 512:1024], op=ALU.add),
                           reads=[b_pt, b_xn_tok], writes=[b_s])

            if dbg and sup == 0:
                DBG("d_s", s_sb[:], b_s)
            if 'F0' not in stages:
                continue
            for tb in range(NTB):
                par = tb % 2
                bst = b_stT[par]
                sx0, sx1 = (0, 1) if par == 0 else (2, 3)
                xfs = SLOT[sx0]; tmp1 = SLOT[sx1]
                bxf, bt1 = b_slot[sx0], b_slot[sx1]
                ytp = vx[:, par * 1024:(par + 1) * 1024]
                EM("vector", lambda e, tb=tb, xfs=xfs, par=par: e.tensor_scalar(
                    out=xfs, in0=xn_v[:, tb, :], scalar1=SC(C_RSTD2 + tb),
                    scalar2=None, op0=ALU.mult),
                   reads=[b_xn_tok, b_stP], writes=[bxf])
                EM("gpsimd", lambda e, xfs=xfs, tmp1=tmp1: e.tensor_mul(tmp1, xfs, fg_bc[:]),
                   reads=[bxf, b_consts], writes=[bt1])
                EM("gpsimd", lambda e, xfs=xfs, tmp1=tmp1: e.tensor_add(xfs, tmp1, fb_bc[:]),
                   reads=[bt1, b_consts], writes=[bxf])
                if dbg and sup == 0 and tb == 0:
                    DBG("d_xf", xfs, bxf)
                EM("vector", lambda e, xfs=xfs, par=par: e.reduce_sum(TC(par, SX), xfs, axis=AX.X),
                   reads=[bxf], writes=[bst])
                EM("scalar", lambda e, xfs=xfs, tmp1=tmp1, par=par: e.activation(
                    out=tmp1, in_=xfs, func=AF.Square, accum_out=TC(par, SQ)),
                   reads=[bxf], writes=[bt1, bst])
                EM("vector", lambda e, par=par: e.tensor_scalar(out=TC(par, MU3), in0=TC(par, SX),
                                                       scalar1=1.0 / D, scalar2=None,
                                                       op0=ALU.mult),
                   reads=[bst], writes=[bst])
                EM("vector", lambda e, par=par: e.tensor_scalar(out=TC(par, T0), in0=TC(par, MU3),
                                                       scalar1=TC(par, MU3),
                                                       scalar2=None, op0=ALU.mult),
                   reads=[bst], writes=[bst])
                EM("vector", lambda e, par=par: e.scalar_tensor_tensor(
                    out=TC(par, T3), in0=TC(par, SQ), scalar=1.0 / D, in1=TC(par, T0),
                    op0=ALU.mult, op1=ALU.subtract),
                   reads=[bst], writes=[bst])
                newton_rsqrt(TC(par, RSTD3), TC(par, T3), 1e-5, par)
                EM("vector", lambda e, par=par: e.tensor_scalar(out=TC(par, NMU), in0=TC(par, MU3),
                                                       scalar1=-1.0, scalar2=None,
                                                       op0=ALU.mult),
                   reads=[bst], writes=[bst])
                EM("vector", lambda e, xfs=xfs, ytp=ytp, par=par: e.tensor_scalar(
                    out=ytp, in0=xfs, scalar1=TC(par, NMU), scalar2=TC(par, RSTD3),
                    op0=ALU.add, op1=ALU.mult),
                   reads=[bxf, bst], writes=[b_vfm[par]])
                ptb = pt if par == 0 else pm3[:].bitcast(BF16)[:, 0:1024]
                bptb = b_pt if par == 0 else b_pm3
                def tpy(e, ytp=ytp, ptb=ptb):
                    for k in range(8):
                        inst = e.transpose(ptb[:, k * 128:(k + 1) * 128],
                                           ytp[:, k * 128:(k + 1) * 128], ident_b[:])
                    return inst
                EM("tensor", tpy, reads=[b_vfm[par], b_consts], writes=[bptb])
                EM("vector", lambda e, tb=tb, ptb=ptb: e.tensor_copy(
                    fm_v[:, :, tb * 128:(tb + 1) * 128],
                    ptb.rearrange("p (k t) -> p k t", k=8)),
                   reads=[bptb], writes=[b_fm])
                EM("gpsimd", lambda e, tb=tb: e.memset(acc_v[:, tb, :], 0.0),
                   writes=[b_acc])
                gbs = gbf[:, 0:E]
                EM("vector", lambda e, tb=tb: e.tensor_copy(
                    gbf[:], gates[:, tb * E:(tb + 1) * E]),
                   reads=[b_gates], writes=[b_gbf])
                EM("tensor", lambda e: e.transpose(pt[0:E, 0:128], gbf[:], ident_b[:]),
                   reads=[b_gbf, b_consts], writes=[b_pt])
                EM("vector", lambda e, tb=tb: e.tensor_copy(
                    gatesT[0:E, tb * 128:(tb + 1) * 128], pt[0:E, 0:128]),
                   reads=[b_pt], writes=[b_gatesT])

            if dbg and sup == 0:
                DBG("d_yfm", fm[:], b_fm)
            if 'F1' not in stages:
                continue
            for ex in range(E):
                for k in range(KD):
                    EM("vector", lambda e, ex=ex, k=k: e.tensor_scalar(
                        out=xe_v[:, k, :], in0=fm_v[:, k, :],
                        scalar1=eg_p[:, ex * KD + k:ex * KD + k + 1],
                        scalar2=eb_p[:, ex * KD + k:ex * KD + k + 1],
                        op0=ALU.mult, op1=ALU.add),
                       reads=[b_fm, b_consts], writes=[b_vx])
                for q in range(4):
                    for hcp in range(4):
                        par = hcp % 2
                        c0 = (q * 8 + hcp * 2) * 128
                        EM("gpsimd", d(w1b_v[par][:],
                                       w1_h[ex, :, c0:c0 + 256].rearrange(
                                           "(k p) h -> p k h", p=128)),
                           writes=[b_w1b[par]], dma=True)
                        EM("gpsimd", d(w2b_v[par][:],
                                       w2_h[ex, :, c0:c0 + 256].rearrange(
                                           "(k p) h -> p k h", p=128)),
                           writes=[b_w2b[par]], dma=True)
                        for hcl in range(2):
                            hcq = hcp * 2 + hcl
                            hcg = q * 8 + hcq
                            p2 = hcq % 2
                            x1p, bx1 = (pm1, b_pm1) if p2 == 0 else (pm3, b_pm3)
                            if p2 == 0:
                                x2ps = [(pm2[:, 0:512], 0), (pm2[:, 512:1024], 512)]
                                bx2 = [b_pm2]
                            else:
                                x2ps = [(ps1[:, 0:512], 0), (pt32[:, 0:512], 512)]
                                bx2 = [b_ps1, b_pt]
                            zf = z2f if p2 == 0 else SLOT[4]
                            sf = slf if p2 == 0 else SLOT[5]
                            hf = htmp if p2 == 0 else htmp2
                            bz = b_z2 if p2 == 0 else b_z2o
                            bs = b_sl if p2 == 0 else b_slo
                            bh = b_htmp if p2 == 0 else b_htmp2
                            bzw = b_slot[4] if p2 == 1 else bz
                            bsw = b_slot[5] if p2 == 1 else bs
                            def x1g(e, par=par, hcl=hcl, x1p=x1p):
                                for sub in range(2):
                                    for k in range(KD):
                                        inst = e.matmul(
                                            x1p[:, sub * 512:(sub + 1) * 512],
                                            w1b_v[par][:, k, hcl * 128:(hcl + 1) * 128],
                                            xe_v[:, k, sub * 512:(sub + 1) * 512],
                                            start=(k == 0), stop=(k == KD - 1))
                                return inst
                            EM("tensor", x1g, reads=[b_w1b[par], b_vx], writes=[bx1])
                            def x2g(e, par=par, hcl=hcl, x2ps=x2ps):
                                for (dst, off) in x2ps:
                                    for k in range(KD):
                                        inst = e.matmul(
                                            dst,
                                            w2b_v[par][:, k, hcl * 128:(hcl + 1) * 128],
                                            xe_v[:, k, off // 512 * 512:
                                                 off // 512 * 512 + 512]
                                            if False else
                                            xe_v[:, k, off:off + 512],
                                            start=(k == 0), stop=(k == KD - 1))
                                return inst
                            EM("tensor", x2g, reads=[b_w2b[par], b_vx], writes=bx2)
                            if p2 == 0:
                                EM("scalar", lambda e, ex=ex, hcg=hcg, zf=zf: e.activation(
                                    out=zf, in_=pm2[:], func=AF.Silu,
                                    bias=b2_p[:, ex * 32 + hcg:ex * 32 + hcg + 1]),
                                   reads=[b_pm2, b_consts], writes=[bzw])
                            else:
                                EM("scalar", lambda e, ex=ex, hcg=hcg, zf=zf: e.activation(
                                    out=zf[:, 0:512], in_=ps1[:, 0:512], func=AF.Silu,
                                    bias=b2_p[:, ex * 32 + hcg:ex * 32 + hcg + 1]),
                                   reads=[b_ps1, b_consts], writes=[bzw])
                                EM("scalar", lambda e, ex=ex, hcg=hcg, zf=zf: e.activation(
                                    out=zf[:, 512:1024], in_=pt32[:, 0:512], func=AF.Silu,
                                    bias=b2_p[:, ex * 32 + hcg:ex * 32 + hcg + 1]),
                                   reads=[b_pt, b_consts], writes=[bzw])
                            EM("vector", lambda e, ex=ex, hcg=hcg, x1p=x1p, zf=zf, hf=hf:
                               e.scalar_tensor_tensor(
                                   out=hf[:], in0=x1p[:],
                                   scalar=b1_p[:, ex * 32 + hcg:ex * 32 + hcg + 1],
                                   in1=zf, op0=ALU.add, op1=ALU.mult),
                               reads=[bx1, bzw, b_consts], writes=[bh])
                            EM("vector", lambda e, hcq=hcq, hf=hf: e.tensor_scalar(
                                out=h_v[:, hcq, :], in0=hf[:], scalar1=-1e4,
                                scalar2=1e4, op0=ALU.max, op1=ALU.min),
                               reads=[bh], writes=[b_h[hcq]])
                            if dbg and sup == 0 and ex == 0 and q == 0 and hcq == 0:
                                DBG("d_z2", z2f, b_z2)
                                DBG("d_h0", h_v[:, 0, :], b_h[0])
                    for grp in range(2):
                        tbs = list(range(grp * 4, grp * 4 + 4))
                        for hcp in range(4):
                            par = hcp % 2
                            r0 = (q * 8 + hcp * 2) * 128
                            EM("gpsimd", d(wob_vv[par][:],
                                           wof_h[ex, r0:r0 + 256, :].rearrange(
                                               "(c p) t -> p c t", p=128)),
                               writes=[b_wob[par]], dma=True)
                            for hcl in range(2):
                                hcq = hcp * 2 + hcl
                                for i, tb in enumerate(tbs):
                                    def og(e, par=par, hcl=hcl, hcq=hcq, i=i, tb=tb):
                                        st = (hcq == 0); sp = (hcq == 7)
                                        if i == 0:
                                            e.matmul(pm1[:, 0:512],
                                                     h_v[:, hcq, tb * 128:(tb + 1) * 128],
                                                     wob_vv[par][:, hcl, 0:512],
                                                     start=st, stop=sp)
                                            return e.matmul(pm1[:, 512:1024],
                                                            h_v[:, hcq, tb * 128:(tb + 1) * 128],
                                                            wob_vv[par][:, hcl, 512:1024],
                                                            start=st, stop=sp)
                                        if i == 1:
                                            e.matmul(pm2[:, 0:512],
                                                     h_v[:, hcq, tb * 128:(tb + 1) * 128],
                                                     wob_vv[par][:, hcl, 0:512],
                                                     start=st, stop=sp)
                                            return e.matmul(pm2[:, 512:1024],
                                                            h_v[:, hcq, tb * 128:(tb + 1) * 128],
                                                            wob_vv[par][:, hcl, 512:1024],
                                                            start=st, stop=sp)
                                        if i == 2:
                                            e.matmul(pm3[:, 0:512],
                                                     h_v[:, hcq, tb * 128:(tb + 1) * 128],
                                                     wob_vv[par][:, hcl, 0:512],
                                                     start=st, stop=sp)
                                            return e.matmul(pm3[:, 512:1024],
                                                            h_v[:, hcq, tb * 128:(tb + 1) * 128],
                                                            wob_vv[par][:, hcl, 512:1024],
                                                            start=st, stop=sp)
                                        e.matmul(ps1[:, 0:512],
                                                 h_v[:, hcq, tb * 128:(tb + 1) * 128],
                                                 wob_vv[par][:, hcl, 0:512],
                                                 start=st, stop=sp)
                                        return e.matmul(pt32[:, 0:512],
                                                        h_v[:, hcq, tb * 128:(tb + 1) * 128],
                                                        wob_vv[par][:, hcl, 512:1024],
                                                        start=st, stop=sp)
                                    wr = [[b_pm1], [b_pm2], [b_pm3], [b_ps1, b_pt]][i]
                                    EM("tensor", og,
                                       reads=[b_h[hcq], b_wob[par]], writes=wr)
                        for i, tb in enumerate(tbs):
                            gcol = gates[:, tb * E + ex:tb * E + ex + 1]
                            if i < 3:
                                pmm = [pm1, pm2, pm3][i]
                                bp = [b_pm1, b_pm2, b_pm3][i]
                                EM("vector", lambda e, tb=tb, pmm=pmm, gcol=gcol:
                                   e.scalar_tensor_tensor(
                                       out=acc_v[:, tb, :], in0=pmm[:], scalar=gcol,
                                       in1=acc_v[:, tb, :], op0=ALU.mult, op1=ALU.add),
                                   reads=[bp, b_gates, b_acc], writes=[b_acc])
                            else:
                                EM("vector", lambda e, tb=tb, gcol=gcol:
                                   e.scalar_tensor_tensor(
                                       out=acc_v[:, tb, 0:512], in0=ps1[:, 0:512],
                                       scalar=gcol, in1=acc_v[:, tb, 0:512],
                                       op0=ALU.mult, op1=ALU.add),
                                   reads=[b_ps1, b_gates, b_acc], writes=[b_acc])
                                EM("vector", lambda e, tb=tb, gcol=gcol:
                                   e.scalar_tensor_tensor(
                                       out=acc_v[:, tb, 512:1024], in0=pt32[:, 0:512],
                                       scalar=gcol, in1=acc_v[:, tb, 512:1024],
                                       op0=ALU.mult, op1=ALU.add),
                                   reads=[b_pt, b_gates, b_acc], writes=[b_acc])

            if dbg and sup == 0:
                DBG("d_acc", g2[:], b_acc)
            if 'F2' not in stages:
                continue
            for tb in range(NTB):
                def gbo(e, tb=tb):
                    e.matmul(pm3[:, 0:512], gatesT[0:E, tb * 128:(tb + 1) * 128],
                             bof_bf[0:E, 0:512], start=True, stop=True)
                    return e.matmul(pm3[:, 512:1024], gatesT[0:E, tb * 128:(tb + 1) * 128],
                                    bof_bf[0:E, 512:1024], start=True, stop=True)
                EM("tensor", gbo, reads=[b_gatesT, b_consts], writes=[b_pm3])
                EM("vector", lambda e, tb=tb: e.tensor_add(SLOT[0], acc_v[:, tb, :], pm3[:]),
                   reads=[b_acc, b_pm3], writes=[b_slot[0]])
                EM("scalar", lambda e: e.activation(out=SLOT[1], in_=SLOT[0], func=AF.Silu),
                   reads=[b_slot[0]], writes=[b_slot[1]])
                xb = xio[tb % 2]; bxb = b_xio[tb % 2]
                EM("sync", d(xb[:], x_h[t0 + tb * 128: t0 + (tb + 1) * 128, :]),
                   writes=[bxb], dma=True)
                EM("gpsimd", lambda e, tb=tb: e.tensor_mul(SLOT[2], s_v[:, tb, :], SLOT[1]),
                   reads=[b_s, b_slot[1]], writes=[b_slot[2]])
                EM("vector", lambda e, xb=xb: e.tensor_add(SLOT[3], SLOT[2], xb[:]),
                   reads=[b_slot[2], bxb], writes=[b_slot[3]])
                EM("sync", lambda e, t0=t0, tb=tb: e.dma_start(
                    out=y_h[t0 + tb * 128: t0 + (tb + 1) * 128, :], in_=SLOT[3]),
                   reads=[b_slot[3]], dma=True, sem="so")

        import os as _os
        if _os.environ.get("SBUF_PROBE"):
            print("sbuf bytes remaining/partition:", nc.sbuf_bytes_remaining)
        P.finalize()
        n_so = P.final_counts.get("so", 0)

        with nc.Block() as block:
            def mk(eng_name):
                ops = P.q[eng_name]
                def body(eng):
                    waited = {}
                    for op in ops:
                        for dpd in op.waits:
                            if waited.get(dpd.sem, 0) < dpd.val:
                                eng.wait_ge(sems[dpd.sem], dpd.val)
                                waited[dpd.sem] = dpd.val
                        inst = op.fn(eng)
                        if op.need_inc:
                            inst.then_inc(sems[op.sem], op.n)
                    if eng_name == "sync":
                        eng.wait_ge(sems["so"], n_so)
                return body
            block.sync(mk("sync"))
            block.gpsimd(mk("gpsimd"))
            block.scalar(mk("scalar"))
            block.vector(mk("vector"))
            block.tensor(mk("tensor"))

    return nc


_NC_CACHE = None


def kernel(**inputs):
    global _NC_CACHE
    ii = {k: np.asarray(v) for k, v in inputs.items()}

    def P(v):
        return np.ascontiguousarray(v, dtype=np.float32)

    com = {
        "wk": P(ii["L0_attn_Wk"]), "wv": P(ii["L0_attn_Wv"]), "wq": P(ii["L0_attn_Wq"]),
        "wkp": P(ii["L0_attn_Wkp"]), "wqp": P(ii["L0_attn_Wqp"]), "woa": P(ii["L0_attn_Wo"]),
        "w1": P(ii["L0_ffn_W1"]), "w2": P(ii["L0_ffn_W2"]), "wof": P(ii["L0_ffn_Wo"]),
        "rw_p": P(ii["L0_ffn_router_w"].reshape(8, 128, 8).transpose(1, 0, 2).reshape(128, 64)),
        "bk_p": P(ii["L0_attn_bk"].reshape(8, 128).T),
        "bq_p": P(ii["L0_attn_bq"].reshape(8, 128).T),
        "bv_p": P(ii["L0_attn_bv"].reshape(8, 128).T),
        "fg_p": P(ii["L0_ffn_ln_g"].reshape(8, 128).T),
        "fb_p": P(ii["L0_ffn_ln_b"].reshape(8, 128).T),
        "eg_p": P(ii["L0_ffn_eg"].reshape(8, 8, 128).transpose(2, 0, 1).reshape(128, 64)),
        "eb_p": P(ii["L0_ffn_eb"].reshape(8, 8, 128).transpose(2, 0, 1).reshape(128, 64)),
        "b1_p": P(ii["L0_ffn_b1"].reshape(8, 32, 128).transpose(2, 0, 1).reshape(128, 256)),
        "b2_p": P(ii["L0_ffn_b2"].reshape(8, 32, 128).transpose(2, 0, 1).reshape(128, 256)),
        "bkp_r": P(ii["L0_attn_bkp"].reshape(1, 1024)),
        "bqp_r": P(ii["L0_attn_bqp"].reshape(1, 1024)),
        "bo_r": P(ii["L0_attn_bo"].reshape(1, 1024)),
        "bof_r": P(ii["L0_ffn_bo"]),
        "ps_bc": P(np.broadcast_to(ii["L0_attn_phase_scale"], (128, 1024))),
        "lng_bc": P(np.broadcast_to(ii["L0_attn_ln_g"], (128, 1024))),
        "lnb_bc": P(np.broadcast_to(ii["L0_attn_ln_b"], (128, 1024))),
        "fg_bc": P(np.broadcast_to(ii["L0_ffn_ln_g"], (128, 1024))),
        "fb_bc": P(np.broadcast_to(ii["L0_ffn_ln_b"], (128, 1024))),
        "ident": np.eye(128, dtype=np.float32),
        "ut2": np.vstack([np.triu(np.ones((64, 64), np.float32))] * 2),
        "consts": P(np.broadcast_to(np.array(
            [PI / 2, 0.5, PI, -PI, PI / 2, -1.5 * PI, 0.0, 0.0], np.float32), (128, 8))),
    }
    x = P(ii["L0_attn_x"])
    in_maps = [dict(com, x=x[c]) for c in range(8)]

    if _NC_CACHE is None:
        _NC_CACHE = build_nc()
    res = run_bass_kernel_spmd(_NC_CACHE, in_maps, core_ids=list(range(8)))
    out = np.stack([np.asarray(r["y"]) for r in res.results], axis=0)
    return out.astype(np.float32)
